# version 07
# baseline (speedup 1.0000x reference)
import contextlib
import numpy as np

import concourse.bass as bass
import concourse.mybir as mybir
from concourse.bass_utils import run_bass_kernel_spmd

F32 = mybir.dt.float32
BF16 = mybir.dt.bfloat16
AF = mybir.ActivationFunctionType
ALU = mybir.AluOpType
AX = mybir.AxisListType.X

D = 2048
T = 2048
NK = 16
NN = 4
EPS = 1e-5


class Sem:
    def __init__(self, h):
        self.h = h
        self.n = 0

    def inc(self, inst, k=1):
        inst.then_inc(self.h, k)
        self.n += k
        return inst

    def dinc(self, inst):
        return self.inc(inst, 16)


def sems(nc, stack, names):
    return {nm: Sem(stack.enter_context(nc.semaphore(nm))) for nm in names}



def emit_ln_stats(nc, S, x_ap, xb, sqb, rows, onc, single, tag):
    x_t = x_ap.rearrange("(k p) t -> k p t", p=128)
    sx = [S[f'{tag}x0'], S[f'{tag}x1']]
    ssq, smm, srow = S[f'{tag}sq'], S[f'{tag}mm'], S[f'{tag}row']
    b_x = [s.n for s in sx]
    b_sq, b_mm = ssq.n, smm.n
    with nc.psum_tensor(f"{tag}_ps_sum_{smm.n}", [1, T], F32) as ps_sum, \
         nc.psum_tensor(f"{tag}_ps_sq_{smm.n}", [1, T], F32) as ps_sq:
        with nc.Block() as blk:
            @blk.sync
            def _(sy):
                for k in range(NK):
                    if k >= 2:
                        sy.wait_ge(smm.h, b_mm + 2 * (k - 2) + 1)
                    sx[k % 2].dinc(sy.dma_start(xb[k % 2][:, :], x_t[k]))
            @blk.scalar
            def _(sc):
                for k in range(NK):
                    sc.wait_ge(sx[k % 2].h, b_x[k % 2] + 16 * (k // 2 + 1))
                    if k >= 1:
                        sc.wait_ge(smm.h, b_mm + 2 * k)
                    ssq.inc(sc.activation(sqb[:, :], xb[k % 2][:, :], AF.Square))
            @blk.tensor
            def _(t):
                for k in range(NK):
                    t.wait_ge(ssq.h, b_sq + k + 1)
                    for n in range(NN):
                        inst = t.matmul(ps_sum[:, n * 512:(n + 1) * 512], onc[:, :],
                                        xb[k % 2][:, n * 512:(n + 1) * 512],
                                        start=(k == 0), stop=(k == NK - 1))
                    smm.inc(inst)
                    for n in range(NN):
                        inst = t.matmul(ps_sq[:, n * 512:(n + 1) * 512], onc[:, :],
                                        sqb[:, n * 512:(n + 1) * 512],
                                        start=(k == 0), stop=(k == NK - 1))
                    smm.inc(inst)
        m, r1, rho, mrho = rows['m'], rows['r1'], rows['rho'], rows['mrho']
        t0, t1 = rows['t0'], rows['t1']
        b_row = srow.n
        with nc.Block() as blk:
            @blk.vector
            def _(v):
                v.wait_ge(smm.h, b_mm + 2 * NK)
                v.tensor_single_scalar(m, ps_sum[:, :], 1.0 / D, ALU.mult)
                v.tensor_single_scalar(t0, ps_sq[:, :], 1.0 / D, ALU.mult)
                v.tensor_mul(t1, m, m)
                v.tensor_sub(t0, t0, t1)
                srow.inc(v.tensor_single_scalar(t1, t0, EPS, ALU.add))
            @blk.scalar
            def _(sc):
                sc.wait_ge(srow.h, b_row + 1)
                sc.activation(t1, t1, AF.Ln)
                srow.inc(sc.activation(r1, t1, AF.Exp, scale=-0.5))
        b_row = srow.n
        with nc.Block() as blk:
            if single:
                @blk.vector
                def _(v):
                    v.wait_ge(srow.h, b_row)
                    v.tensor_copy(rho, r1)
                    srow.inc(v.tensor_mul(mrho, m, rho))
            else:
                @blk.vector
                def _(v):
                    v.wait_ge(srow.h, b_row)
                    v.tensor_mul(t1, t0, r1)
                    v.tensor_mul(t1, t1, r1)
                    srow.inc(v.tensor_single_scalar(t1, t1, EPS, ALU.add))
                @blk.scalar
                def _(sc):
                    sc.wait_ge(srow.h, b_row + 1)
                    sc.activation(t1, t1, AF.Ln)
                    srow.inc(sc.activation(t1, t1, AF.Exp, scale=-0.5))
        if not single:
            b_row = srow.n
            with nc.Block() as blk:
                @blk.vector
                def _(v):
                    v.wait_ge(srow.h, b_row)
                    v.tensor_mul(rho, r1, t1)
                    srow.inc(v.tensor_mul(mrho, m, rho))
    nc.all_engine_barrier()


def emit_bcast(nc, S, row_dest_pairs, onr, tag):
    sbc, sbe = S[f'{tag}bc'], S[f'{tag}bce']
    b_bc, b_be = sbc.n, sbe.n
    with nc.psum_tensor(f"bc_ps_{sbc.n}", [128, T], F32) as ps_bc:
        with nc.Block() as blk:
            @blk.tensor
            def _(t):
                t.wait_ge(S[f'{tag}row'].h, S[f'{tag}row'].n)
                for i, (row, dest) in enumerate(row_dest_pairs):
                    if i > 0:
                        t.wait_ge(sbe.h, b_be + i)
                    for n in range(NN):
                        inst = t.matmul(ps_bc[:, n * 512:(n + 1) * 512], onr[:, :],
                                        row[:, n * 512:(n + 1) * 512],
                                        start=True, stop=True)
                    sbc.inc(inst)
            @blk.vector
            def _(v):
                for i, (row, dest) in enumerate(row_dest_pairs):
                    v.wait_ge(sbc.h, b_bc + i + 1)
                    sbe.inc(v.tensor_copy(dest[:, :], ps_bc[:, :]))
    nc.all_engine_barrier()


def emit_ln_apply(nc, S, x_ap, xb, rhob, mrhob, xn_sb, gamma, beta, tag,
                  half=None, seg=None):
    if seg is not None:
        off, W = seg
    else:
        W = T if half is None else T // 2
        off = 0 if (half is None or half == 0) else T // 2
    x_t = x_ap.rearrange("(k p) t -> k p t", p=128)
    sx = [S[f'{tag}x0'], S[f'{tag}x1']]
    sap = S[f'{tag}ap']
    b_x = [s.n for s in sx]
    b_ap = sap.n
    with nc.Block() as blk:
        @blk.sync
        def _(sy):
            for k in range(NK):
                if k >= 2:
                    sy.wait_ge(sap.h, b_ap + (k - 1))
                sx[k % 2].dinc(sy.dma_start(xb[k % 2][:, 0:W], x_t[k, :, off:off + W]))
        @blk.vector
        def _(v):
            v.wait_ge(S[f'{tag}bce'].h, S[f'{tag}bce'].n)
            for k in range(NK):
                v.wait_ge(sx[k % 2].h, b_x[k % 2] + 16 * (k // 2 + 1))
                v.tensor_mul(xb[k % 2][:, 0:W], xb[k % 2][:, 0:W],
                             rhob[:, off:off + W])
                o = xn_sb[:, k * W:(k + 1) * W]
                if gamma is not None:
                    v.tensor_sub(xb[k % 2][:, 0:W], xb[k % 2][:, 0:W],
                                 mrhob[:, off:off + W])
                    sap.inc(v.tensor_scalar(o, xb[k % 2][:, 0:W],
                                            gamma[:, k:k + 1], beta[:, k:k + 1],
                                            ALU.mult, ALU.add))
                else:
                    sap.inc(v.tensor_sub(o, xb[k % 2][:, 0:W],
                                         mrhob[:, off:off + W]))
    nc.all_engine_barrier()



def emit_sec_A(nc, S, sh, dr, debug=False):
    xb = sh['xb']; sqb = sh['sqb']; sq2 = sh['sq2']
    onc = sh['onc']; onr = sh['onr']; half_sb = sh.get('half_sb')
    st = contextlib.ExitStack()

    xT = dr["xT"]
    w_in = dr["w_in"]
    b_in = dr["b_in"]
    w_out = dr["w_out"]
    b_out = dr["b_out"]
    gamma = dr["gamma"]
    beta = dr["beta"]
    cosT = dr["cosT"]
    sinT = dr["sinT"]
    ones_col = dr["ones_col"]
    ones_row = dr["ones_row"]
    x1T = dr["x1T"]
    q_s = dr["qa_s"]
    k_s = dr["ka_s"]
    v_s = dr["va_s"]
    if debug:
        dbg_A = dr["dbg_A"]

    row_m = st.enter_context(nc.sbuf_tensor("a_row_m", [1, T], F32))
    row_r1 = st.enter_context(nc.sbuf_tensor("a_row_r1", [1, T], F32))
    row_rho = st.enter_context(nc.sbuf_tensor("a_row_rho", [1, T], F32))
    row_mrho = st.enter_context(nc.sbuf_tensor("a_row_mrho", [1, T], F32))
    rows = {'m': row_m[0:1, :], 'r1': row_r1[0:1, :], 'rho': row_rho[0:1, :],
            'mrho': row_mrho[0:1, :], 't0': sqb[0:1, 0:T], 't1': sq2[0:1, 0:T]}
    bc0 = st.enter_context(nc.sbuf_tensor("a_bc0", [128, T], F32))
    bc1 = st.enter_context(nc.sbuf_tensor("a_bc1", [128, T], F32))
    bi_sb = st.enter_context(nc.sbuf_tensor("a_bi_sb", [128, 48], F32))
    bo_sb = st.enter_context(nc.sbuf_tensor("a_bo_sb", [128, 16], F32))
    ga_sb = st.enter_context(nc.sbuf_tensor("a_ga_sb", [128, 16], F32))
    be_sb = st.enter_context(nc.sbuf_tensor("a_be_sb", [128, 16], F32))
    A_sb = st.enter_context(nc.sbuf_tensor("a_A_sb", [128, 16], F32))


    with nc.Block() as blk:
        @blk.sync
        def _(sy):
            for dst, src in ((onc, ones_col), (onr, ones_row), (bi_sb, b_in),
                             (bo_sb, b_out), (ga_sb, gamma), (be_sb, beta)):
                S['mi0'].dinc(sy.dma_start(dst[:, :], src))
            sy.wait_ge(S['mi0'].h, S['mi0'].n)
        @blk.vector
        def _(v):
            v.memset(half_sb[:, :], 0.5)
    nc.all_engine_barrier()

    emit_ln_stats(nc, S, xT, xb, sqb, rows, onc, single=False, tag='p')
    emit_bcast(nc, S, [(rows['rho'], bc0), (rows['mrho'], bc1)], onr, tag='p')

    a2 = contextlib.ExitStack()
    xn_sb = a2.enter_context(nc.sbuf_tensor("a_xn_sb", [128, NK * T], BF16))
    emit_ln_apply(nc, S, xT, xb, bc0, bc1, xn_sb, ga_sb, be_sb, tag='p')

    wbuf = [a2.enter_context(nc.sbuf_tensor(f"a_wbuf{i}", [128, NK * 128], BF16))
            for i in range(2)]
    cs_sb = a2.enter_context(nc.sbuf_tensor("a_cs_sb", [128, T], F32))
    sn_sb = a2.enter_context(nc.sbuf_tensor("a_sn_sb", [128, T], F32))
    e1 = xb[0]
    e2 = xb[1]
    qsl = [a2.enter_context(nc.sbuf_tensor(f"a_qsl{i}", [128, T], BF16)) for i in range(2)]
    psl = a2.enter_context(nc.sbuf_tensor("a_psl", [128, T], F32))
    rt1 = a2.enter_context(nc.sbuf_tensor("a_rt1", [128, T], F32))
    rt2 = a2.enter_context(nc.sbuf_tensor("a_rt2", [128, T], F32))

    sched = [('q', i, i) for i in range(16)]
    for c in range(4):
        sched += [('p1', 16 + c, ('k', c)), ('p2', 20 + c, ('k', c)),
                  ('p1', 32 + c, ('v', c)), ('p2', 36 + c, ('v', c))]
    sched += [('pass', 24 + i, ('k', 8 + i)) for i in range(8)]
    sched += [('pass', 40 + i, ('v', 8 + i)) for i in range(8)]
    n_tiles = len(sched)
    pair_of = {}
    pr = 0
    for mi, (kind, wi, ex) in enumerate(sched):
        if kind in ('p1', 'p2'):
            pair_of[mi] = pr
            if kind == 'p2':
                pr += 1
    n_pairs = pr
    p2_mi = [mi for mi in range(n_tiles) if sched[mi][0] == 'p2']

    qs_t = q_s.rearrange("(k p) t -> k p t", p=128)
    ks_t = k_s.rearrange("(k p) t -> k p t", p=128)
    vs_t = v_s.rearrange("(k p) t -> k p t", p=128)
    cos_t = cosT.rearrange("(c p) t -> c p t", p=128)
    sin_t = sinT.rearrange("(c p) t -> c p t", p=128)

    b = {k: S[k].n for k in ('w0', 'w1', 'mmt', 'evt', 'rot', 'cs0',
                             'q0', 'q1', 'k0', 'r0', 'r1s')}
    with nc.psum_tensor("a_psq0", [128, T], F32) as psq0, \
         nc.psum_tensor("a_psq1", [128, T], F32) as psq1:
        psq = [psq0, psq1]
        with nc.Block() as blk:
            @blk.gpsimd
            def _(g):
                for mi in range(n_tiles):
                    if mi >= 2:
                        g.wait_ge(S['mmt'].h, b['mmt'] + mi - 1)
                    S[f'w{mi % 2}'].dinc(g.dma_start(
                        wbuf[mi % 2][:, :].rearrange("p (k c) -> p k c", c=128),
                        w_in[sched[mi][1]].rearrange("(k p) c -> p k c", p=128)))

            @blk.tensor
            def _(t):
                t.wait_ge(S['pap'].h, S['pap'].n)
                for mi in range(n_tiles):
                    t.wait_ge(S[f'w{mi % 2}'].h, b[f'w{mi % 2}'] + 16 * (mi // 2 + 1))
                    if mi >= 2:
                        t.wait_ge(S['evt'].h, b['evt'] + mi - 1)
                    for n in range(NN):
                        for k in range(NK):
                            inst = t.matmul(
                                psq[mi % 2][:, n * 512:(n + 1) * 512],
                                wbuf[mi % 2][:, k * 128:(k + 1) * 128],
                                xn_sb[:, k * T + n * 512:k * T + (n + 1) * 512],
                                start=(k == 0), stop=(k == NK - 1))
                    S['mmt'].inc(inst)

            @blk.scalar
            def _(sc):
                nq = 0
                npass = 0
                for mi in range(n_tiles):
                    kind, wi, ex = sched[mi]
                    sc.wait_ge(S['mmt'].h, b['mmt'] + mi + 1)
                    bias = bi_sb[:, wi:wi + 1]
                    if kind == 'q':
                        if nq >= 2:
                            sc.wait_ge(S[f'q{nq % 2}'].h, b[f'q{nq % 2}'] + 16 * (nq // 2))
                        S['evt'].inc(sc.activation(qsl[nq % 2][:, :], psq[mi % 2][:, :],
                                                   AF.Identity, bias=bias))
                        nq += 1
                    elif kind == 'p1':
                        p = pair_of[mi]
                        if p >= 1:
                            sc.wait_ge(S['rot'].h, b['rot'] + p)
                        S['evt'].inc(sc.activation(e1[:, :], psq[mi % 2][:, :],
                                                   AF.Identity, bias=bias))
                    elif kind == 'p2':
                        S['evt'].inc(sc.activation(e2[:, :], psq[mi % 2][:, :],
                                                   AF.Identity, bias=bias))
                    else:
                        if npass >= 1:
                            sc.wait_ge(S['k0'].h, b['k0'] + 16 * npass)
                        S['evt'].inc(sc.activation(psl[:, :], psq[mi % 2][:, :],
                                                   AF.Identity, bias=bias))
                        npass += 1

            @blk.vector
            def _(v):
                for p in range(n_pairs):
                    c = sched[p2_mi[p]][2][1]
                    v.wait_ge(S['evt'].h, b['evt'] + p2_mi[p] + 1)
                    v.wait_ge(S['cs0'].h, b['cs0'] + 32 * (c + 1))
                    if p >= 1:
                        v.wait_ge(S['r0'].h, b['r0'] + 16 * p)
                        v.wait_ge(S['r1s'].h, b['r1s'] + 16 * p)
                    v.tensor_mul(rt1[:, :], e1[:, :], cs_sb[:, :])
                    v.tensor_mul(sqb[:, :], e2[:, :], sn_sb[:, :])
                    v.tensor_sub(rt1[:, :], rt1[:, :], sqb[:, :])
                    v.tensor_mul(rt2[:, :], e2[:, :], cs_sb[:, :])
                    v.tensor_mul(sqb[:, :], e1[:, :], sn_sb[:, :])
                    S['rot'].inc(v.tensor_add(rt2[:, :], rt2[:, :], sqb[:, :]))

            @blk.sync
            def _(sy):
                nq = 0
                npass = 0
                for mi in range(n_tiles):
                    kind, wi, ex = sched[mi]
                    if kind == 'p1' and ex[0] == 'k':
                        c = ex[1]
                        if c > 0:
                            sy.wait_ge(S['rot'].h, b['rot'] + 2 * c)
                        S['cs0'].dinc(sy.dma_start(cs_sb[:, :], cos_t[c]))
                        S['cs0'].dinc(sy.dma_start(sn_sb[:, :], sin_t[c]))
                    if kind == 'q':
                        sy.wait_ge(S['evt'].h, b['evt'] + mi + 1)
                        S[f'q{nq % 2}'].dinc(sy.dma_start(qs_t[ex], qsl[nq % 2][:, :]))
                        nq += 1
                    elif kind == 'p2':
                        tgt, c = ex
                        p = pair_of[mi]
                        sy.wait_ge(S['rot'].h, b['rot'] + p + 1)
                        dst = ks_t if tgt == 'k' else vs_t
                        S['r0'].dinc(sy.dma_start(dst[c], rt1[:, :]))
                        S['r1s'].dinc(sy.dma_start(dst[c + 4], rt2[:, :]))
                    elif kind == 'pass':
                        tgt, ci = ex
                        sy.wait_ge(S['evt'].h, b['evt'] + mi + 1)
                        dst = ks_t if tgt == 'k' else vs_t
                        S['k0'].dinc(sy.dma_start(dst[ci], psl[:, :]))
                        npass += 1
                for s in ('q0', 'q1', 'k0', 'r0', 'r1s'):
                    sy.wait_ge(S[s].h, S[s].n)
    nc.all_engine_barrier()
    a2.close()

    a3 = contextlib.ExitStack()
    kc = [a3.enter_context(nc.sbuf_tensor(f"a_kc{i}", [128, T], F32)) for i in range(2)]
    vc = [a3.enter_context(nc.sbuf_tensor(f"a_vc{i}", [128, T], F32)) for i in range(2)]
    b = {k: S[k].n for k in ('px0', 'px1', 'psq', 'pmm', 'prow')}
    with nc.psum_tensor("a_ps_ssk", [1, T], F32) as ps_ssk, \
         nc.psum_tensor("a_ps_ssv", [1, T], F32) as ps_ssv:
        with nc.Block() as blk:
            @blk.sync
            def _(sy):
                for k in range(NK):
                    if k >= 2:
                        sy.wait_ge(S['psq'].h, b['psq'] + 2 * (k - 1))
                    S[f'px{k % 2}'].dinc(sy.dma_start(kc[k % 2][:, :], ks_t[k]))
                    S[f'px{k % 2}'].dinc(sy.dma_start(vc[k % 2][:, :], vs_t[k]))
            @blk.scalar
            def _(sc):
                for k in range(NK):
                    sc.wait_ge(S[f'px{k % 2}'].h, b[f'px{k % 2}'] + 32 * (k // 2 + 1))
                    if k >= 1:
                        sc.wait_ge(S['pmm'].h, b['pmm'] + 2 * k)
                    S['psq'].inc(sc.activation(sqb[:, :], kc[k % 2][:, :], AF.Square))
                    S['psq'].inc(sc.activation(sq2[:, :], vc[k % 2][:, :], AF.Square))
            @blk.tensor
            def _(t):
                for k in range(NK):
                    t.wait_ge(S['psq'].h, b['psq'] + 2 * k + 2)
                    for n in range(NN):
                        inst = t.matmul(ps_ssk[:, n * 512:(n + 1) * 512], onc[:, :],
                                        sqb[:, n * 512:(n + 1) * 512],
                                        start=(k == 0), stop=(k == NK - 1))
                    S['pmm'].inc(inst)
                    for n in range(NN):
                        inst = t.matmul(ps_ssv[:, n * 512:(n + 1) * 512], onc[:, :],
                                        sq2[:, n * 512:(n + 1) * 512],
                                        start=(k == 0), stop=(k == NK - 1))
                    S['pmm'].inc(inst)
        b_row = S['prow'].n
        with nc.Block() as blk:
            @blk.vector
            def _(v):
                v.wait_ge(S['pmm'].h, S['pmm'].n)
                v.tensor_copy(rows['t1'], ps_ssk[:, :])
                S['prow'].inc(v.tensor_tensor(rows['t0'], rows['t1'],
                                              ps_ssv[:, :], ALU.mult))
            @blk.scalar
            def _(sc):
                sc.wait_ge(S['prow'].h, b_row + 1)
                sc.activation(rows['t1'], rows['t0'], AF.Ln)
                S['prow'].inc(sc.activation(rows['t0'], rows['t1'],
                                            AF.Exp, scale=-0.5))
    nc.all_engine_barrier()
    emit_bcast(nc, S, [(rows['t0'], bc0)], onr, tag='p')

    b = {k: S[k].n for k in ('px0', 'px1', 'pap')}
    with nc.Block() as blk:
        @blk.sync
        def _(sy):
            for k in range(NK):
                if k >= 2:
                    sy.wait_ge(S['pap'].h, b['pap'] + (k - 1))
                S[f'px{k % 2}'].dinc(sy.dma_start(kc[k % 2][:, :], ks_t[k]))
                S[f'px{k % 2}'].dinc(sy.dma_start(vc[k % 2][:, :], vs_t[k]))
        @blk.vector
        def _(v):
            v.wait_ge(S['pbce'].h, S['pbce'].n)
            for k in range(NK):
                v.wait_ge(S[f'px{k % 2}'].h, b[f'px{k % 2}'] + 32 * (k // 2 + 1))
                v.tensor_mul(sqb[:, :], kc[k % 2][:, :], vc[k % 2][:, :])
                v.tensor_mul(sqb[:, :], sqb[:, :], bc0[:, :])
                S['pap'].inc(v.reduce_sum(A_sb[:, k:k + 1], sqb[:, :], axis=AX))
        @blk.scalar
        def _(sc):
            sc.wait_ge(S['pap'].h, b['pap'] + NK)
            S['prow'].inc(sc.activation(A_sb[:, :], A_sb[:, :], AF.Identity,
                                        bias=half_sb[:, 0:1], scale=0.5))
    nc.all_engine_barrier()
    a3.close()
    if debug:
        with nc.Block() as blk:
            @blk.sync
            def _(sy):
                S['mi0'].dinc(sy.dma_start(dbg_A, A_sb[:, :]))
                sy.wait_ge(S['mi0'].h, S['mi0'].n)
        nc.all_engine_barrier()

    a4 = contextlib.ExitStack()
    wt_sb = a4.enter_context(nc.sbuf_tensor("a_wt_sb", [128, NK * T], BF16))
    q_sb = a4.enter_context(nc.sbuf_tensor("a_q_sb", [128, NK * (T // 2)], BF16))
    s1b = [a4.enter_context(nc.sbuf_tensor(f"a_s1b{i}", [128, T // 2], F32))
           for i in range(2)]
    xb2 = [a4.enter_context(nc.sbuf_tensor(f"a_xb2{i}", [128, T // 2], F32))
           for i in range(2)]
    emit_bcast(nc, S, [(rows['m'], bc0), (rows['r1'], bc1)], onr, tag='p')

    wo_t = w_out.rearrange("(k p) t -> k p t", p=128)
    x_t_in = xT.rearrange("(k p) t -> k p t", p=128)
    x1_t = x1T.rearrange("(k p) t -> k p t", p=128)
    W2 = T // 2

    b = {k: S[k].n for k in ('w0', 'w1', 'pap')}
    with nc.Block() as blk:
        @blk.gpsimd
        def _(g):
            for k in range(NK):
                if k >= 2:
                    g.wait_ge(S['pap'].h, b['pap'] + (k - 1))
                S[f'w{k % 2}'].dinc(g.dma_start(xb[k % 2][:, :], wo_t[k]))
        @blk.vector
        def _(v):
            v.wait_ge(S['prow'].h, S['prow'].n)
            for k in range(NK):
                v.wait_ge(S[f'w{k % 2}'].h, b[f'w{k % 2}'] + 16 * (k // 2 + 1))
                S['pap'].inc(v.tensor_single_scalar(
                    wt_sb[:, k * T:(k + 1) * T], xb[k % 2][:, :],
                    A_sb[:, k:k + 1], ALU.mult))
    nc.all_engine_barrier()

    for h in range(2):
        off = h * W2
        b = {k: S[k].n for k in ('px0', 'px1', 'mmt', 'evt', 'dv',
                                 'o0', 'o1', 'q0', 'q1')}
        with nc.psum_tensor(f"a_psA0_{h}", [128, W2], F32) as psA0, \
             nc.psum_tensor(f"a_psA1_{h}", [128, W2], F32) as psA1:
            psA = [psA0, psA1]
            with nc.Block() as blk:
                @blk.gpsimd
                def _(g):
                    for k in range(NK):
                        S[f'q{k % 2}'].dinc(g.dma_start(
                            q_sb[:, k * W2:(k + 1) * W2], qs_t[k, :, off:off + W2]))
                @blk.tensor
                def _(t):
                    t.wait_ge(S['pap'].h, S['pap'].n)
                    t.wait_ge(S['q0'].h, b['q0'] + 16 * 8)
                    t.wait_ge(S['q1'].h, b['q1'] + 16 * 8)
                    for m in range(NK):
                        if m >= 2:
                            t.wait_ge(S['evt'].h, b['evt'] + m - 1)
                        for n in range(2):
                            for k in range(NK):
                                inst = t.matmul(
                                    psA[m % 2][:, n * 512:(n + 1) * 512],
                                    wt_sb[:, k * T + m * 128:k * T + (m + 1) * 128],
                                    q_sb[:, k * W2 + n * 512:k * W2 + (n + 1) * 512],
                                    start=(k == 0), stop=(k == NK - 1))
                        S['mmt'].inc(inst)
                @blk.scalar
                def _(sc):
                    for m in range(NK):
                        sc.wait_ge(S['mmt'].h, b['mmt'] + m + 1)
                        if m >= 2:
                            sc.wait_ge(S[f'o{m % 2}'].h, b[f'o{m % 2}'] + 16 * (m // 2))
                        S['evt'].inc(sc.activation(s1b[m % 2][:, :], psA[m % 2][:, :],
                                                   AF.Identity, bias=bo_sb[:, m:m + 1]))
                @blk.vector
                def _(v):
                    for m in range(NK):
                        v.wait_ge(S['evt'].h, b['evt'] + m + 1)
                        v.wait_ge(S[f'px{m % 2}'].h, b[f'px{m % 2}'] + 16 * (m // 2 + 1))
                        v.tensor_sub(sqb[:, 0:W2], xb2[m % 2][:, :], bc0[:, off:off + W2])
                        v.tensor_mul(sqb[:, 0:W2], sqb[:, 0:W2], bc1[:, off:off + W2])
                        v.tensor_add(sqb[:, 0:W2], sqb[:, 0:W2], xb2[m % 2][:, :])
                        S['dv'].inc(v.tensor_add(s1b[m % 2][:, :], sqb[:, 0:W2],
                                                 s1b[m % 2][:, :]))
                @blk.sync
                def _(sy):
                    S['px0'].dinc(sy.dma_start(xb2[0][:, :], x_t_in[0, :, off:off + W2]))
                    S['px1'].dinc(sy.dma_start(xb2[1][:, :], x_t_in[1, :, off:off + W2]))
                    for m in range(NK):
                        sy.wait_ge(S['dv'].h, b['dv'] + m + 1)
                        S[f'o{m % 2}'].dinc(sy.dma_start(x1_t[m, :, off:off + W2],
                                                         s1b[m % 2][:, :]))
                        if m + 2 < NK:
                            S[f'px{m % 2}'].dinc(sy.dma_start(
                                xb2[m % 2][:, :], x_t_in[m + 2, :, off:off + W2]))
                    for s in ('o0', 'o1'):
                        sy.wait_ge(S[s].h, S[s].n)
        nc.all_engine_barrier()
    a4.close()

    st.close()
    nc.all_engine_barrier()




def emit_sec_B(nc, S, sh, dr, debug=False):
    xb = sh['xb']; sqb = sh['sqb']; sq2 = sh['sq2']
    onc = sh['onc']; onr = sh['onr']; half_sb = sh.get('half_sb')
    st = contextlib.ExitStack()

    x1T = dr["x1T"]
    w = dr["w"]
    lnw = dr["lnw"]
    lnb = dr["lnb"]
    ones_col = dr["ones_col"]
    ones_row = dr["ones_row"]
    x2T = dr["x2T"]
    if debug:
        dbg_wq = dr["dbg_wq"]
        dbg_wabs = dr["dbg_wabs"]
        dbg_thr = dr["dbg_thr"]

    row_m = st.enter_context(nc.sbuf_tensor("b_row_m", [1, T], F32))
    row_r1 = st.enter_context(nc.sbuf_tensor("b_row_r1", [1, T], F32))
    row_rho = st.enter_context(nc.sbuf_tensor("b_row_rho", [1, T], F32))
    row_mrho = st.enter_context(nc.sbuf_tensor("b_row_mrho", [1, T], F32))
    rows = {'m': row_m[0:1, :], 'r1': row_r1[0:1, :], 'rho': row_rho[0:1, :],
            'mrho': row_mrho[0:1, :], 't0': sqb[0:1, 0:T], 't1': sq2[0:1, 0:T]}
    lw_sb = st.enter_context(nc.sbuf_tensor("b_lw_sb", [128, 16], F32))
    lb_sb = st.enter_context(nc.sbuf_tensor("b_lb_sb", [128, 16], F32))
    wabs = st.enter_context(nc.sbuf_tensor("b_wabs", [128, 16], F32))
    thr_b = st.enter_context(nc.sbuf_tensor("b_thr_b", [128, 1], F32))


    with nc.Block() as blk:
        @blk.sync
        def _(sy):
            for dst, src in ((onc, ones_col), (onr, ones_row),
                             (lw_sb, lnw), (lb_sb, lnb)):
                S['mi0'].dinc(sy.dma_start(dst[:, :], src))
            sy.wait_ge(S['mi0'].h, S['mi0'].n)
    nc.all_engine_barrier()

    emit_ln_stats(nc, S, x1T, xb, sqb, rows, onc, single=False, tag='p')

    xn_sb = st.enter_context(nc.sbuf_tensor("b_xn_sb", [128, NK * T], BF16))
    bstk = contextlib.ExitStack()
    bc0 = bstk.enter_context(nc.sbuf_tensor("b_bc0", [128, T], F32))
    bc1 = bstk.enter_context(nc.sbuf_tensor("b_bc1", [128, T], F32))
    emit_bcast(nc, S, [(rows['rho'], bc0), (rows['mrho'], bc1)], onr, tag='p')
    emit_ln_apply(nc, S, x1T, xb, bc0, bc1, xn_sb, lw_sb, lb_sb, tag='p')

    w_t = w.rearrange("(k p) t -> k p t", p=128)
    b = {k: S[k].n for k in ('w0', 'w1', 'dv', 'pmm', 'prow', 'pbc')}
    with nc.psum_tensor("b_psS", [1, 16], F32) as psS, \
         nc.psum_tensor("b_psB", [128, 1], F32) as psB:
        with nc.Block() as blk:
            @blk.gpsimd
            def _(g):
                for k in range(NK):
                    if k >= 2:
                        g.wait_ge(S['dv'].h, b['dv'] + (k - 1))
                    S[f'w{k % 2}'].dinc(g.dma_start(xb[k % 2][:, :], w_t[k]))
            @blk.vector
            def _(v):
                for k in range(NK):
                    v.wait_ge(S[f'w{k % 2}'].h, b[f'w{k % 2}'] + 16 * (k // 2 + 1))
                    S['dv'].inc(v.tensor_reduce(wabs[:, k:k + 1], xb[k % 2][:, :],
                                                AX, ALU.add, apply_absolute_value=True))
            @blk.tensor
            def _(t):
                t.wait_ge(S['dv'].h, b['dv'] + NK)
                S['pmm'].inc(t.matmul(psS[:, 0:16], onc[:, :], wabs[:, :],
                                      start=True, stop=True))
        with nc.Block() as blk:
            @blk.vector
            def _(v):
                v.wait_ge(S['pmm'].h, b['pmm'] + 1)
                S['prow'].inc(v.reduce_sum(rows['t0'][0:1, 0:1],
                                           psS[:, 0:16], axis=AX))
        with nc.Block() as blk:
            @blk.tensor
            def _(t):
                t.wait_ge(S['prow'].h, S['prow'].n)
                S['pbc'].inc(t.matmul(psB[:, :], onr[:, :], rows['t0'][0:1, 0:1],
                                      start=True, stop=True))
            @blk.vector
            def _(v):
                v.wait_ge(S['pbc'].h, S['pbc'].n)
                S['prow'].inc(v.tensor_copy(thr_b[:, :], psB[:, :]))
    nc.all_engine_barrier()

    bstk.close()
    wq_sb = st.enter_context(nc.sbuf_tensor("b_wq_sb", [128, NK * T], BF16))
    sgn = st.enter_context(nc.sbuf_tensor("b_sgn", [128, T], F32))
    b = {k: S[k].n for k in ('w0', 'w1', 'psq', 'qz')}
    with nc.Block() as blk:
        @blk.gpsimd
        def _(g):
            for k in range(NK):
                if k >= 2:
                    g.wait_ge(S['qz'].h, b['qz'] + (k - 1))
                S[f'w{k % 2}'].dinc(g.dma_start(xb[k % 2][:, :], w_t[k]))
        @blk.scalar
        def _(sc):
            for k in range(NK):
                sc.wait_ge(S[f'w{k % 2}'].h, b[f'w{k % 2}'] + 16 * (k // 2 + 1))
                if k >= 1:
                    sc.wait_ge(S['qz'].h, b['qz'] + k)
                S['psq'].inc(sc.activation(sqb[:, :], xb[k % 2][:, :], AF.Abs,
                                           scale=2.0 * 2048.0 * 2048.0))
                S['psq'].inc(sc.sign(sgn[:, :], xb[k % 2][:, :]))
        @blk.vector
        def _(v):
            v.wait_ge(S['prow'].h, S['prow'].n)
            for k in range(NK):
                v.wait_ge(S['psq'].h, b['psq'] + 2 * k + 2)
                v.tensor_single_scalar(sq2[:, :], sqb[:, :], thr_b[:, 0:1], ALU.is_ge)
                S['qz'].inc(v.tensor_mul(wq_sb[:, k * T:(k + 1) * T], sq2[:, :],
                                         sgn[:, :]))
    nc.all_engine_barrier()
    if debug:
        with nc.Block() as blk:
            @blk.sync
            def _(sy):
                S['mi0'].dinc(sy.dma_start(dbg_wabs, wabs[:, :]))
                S['mi0'].dinc(sy.dma_start(dbg_thr, thr_b[:, :]))
                sy.wait_ge(S['mi0'].h, S['mi0'].n)
        nc.all_engine_barrier()
        bdg = {k: S[k].n for k in ('mi0', 'o0')}
        with nc.Block() as blk:
            @blk.vector
            def _(v):
                for k in range(NK):
                    if k >= 1:
                        v.wait_ge(S['o0'].h, bdg['o0'] + 16 * k)
                    S['mi0'].inc(v.tensor_copy(sqb[:, :], wq_sb[:, k * T:(k + 1) * T]))
            @blk.sync
            def _(sy):
                for k in range(NK):
                    sy.wait_ge(S['mi0'].h, bdg['mi0'] + k + 1)
                    S['o0'].dinc(sy.dma_start(
                        dbg_wq.rearrange("(k p) t -> k p t", p=128)[k], sqb[:, :]))
                sy.wait_ge(S['o0'].h, S['o0'].n)
        nc.all_engine_barrier()

    x1_t = x1T.rearrange("(k p) t -> k p t", p=128)
    x2_t = x2T.rearrange("(k p) t -> k p t", p=128)
    b = {k: S[k].n for k in ('px0', 'px1', 'mmt', 'dv', 'o0', 'o1')}
    osl = [sqb, sq2]
    with nc.psum_tensor("b_psB0", [128, T], F32) as psB0, \
         nc.psum_tensor("b_psB1", [128, T], F32) as psB1:
        psB4 = [psB0, psB1]
        with nc.Block() as blk:
            @blk.tensor
            def _(t):
                t.wait_ge(S['qz'].h, S['qz'].n)
                t.wait_ge(S['pap'].h, S['pap'].n)
                for m in range(NK):
                    if m >= 2:
                        t.wait_ge(S['dv'].h, b['dv'] + m - 1)
                    for n in range(NN):
                        for k in range(NK):
                            inst = t.matmul(
                                psB4[m % 2][:, n * 512:(n + 1) * 512],
                                wq_sb[:, k * T + m * 128:k * T + (m + 1) * 128],
                                xn_sb[:, k * T + n * 512:k * T + (n + 1) * 512],
                                start=(k == 0), stop=(k == NK - 1))
                    S['mmt'].inc(inst)
            @blk.vector
            def _(v):
                for m in range(NK):
                    v.wait_ge(S['mmt'].h, b['mmt'] + m + 1)
                    v.wait_ge(S[f'px{m % 2}'].h, b[f'px{m % 2}'] + 16 * (m // 2 + 1))
                    if m >= 2:
                        v.wait_ge(S[f'o{m % 2}'].h, b[f'o{m % 2}'] + 16 * (m // 2))
                    S['dv'].inc(v.tensor_tensor(osl[m % 2][:, :], psB4[m % 2][:, :],
                                                xb[m % 2][:, :], ALU.add))
            @blk.sync
            def _(sy):
                S['px0'].dinc(sy.dma_start(xb[0][:, :], x1_t[0]))
                S['px1'].dinc(sy.dma_start(xb[1][:, :], x1_t[1]))
                for m in range(NK):
                    sy.wait_ge(S['dv'].h, b['dv'] + m + 1)
                    S[f'o{m % 2}'].dinc(sy.dma_start(x2_t[m], osl[m % 2][:, :]))
                    if m + 2 < NK:
                        S[f'px{m % 2}'].dinc(sy.dma_start(xb[m % 2][:, :], x1_t[m + 2]))
                for s in ('o0', 'o1'):
                    sy.wait_ge(S[s].h, S[s].n)
    nc.all_engine_barrier()
    st.close()
    nc.all_engine_barrier()



def emit_sec_C(nc, S, sh, dr, debug=False):
    xb = sh['xb']; sqb = sh['sqb']; sq2 = sh['sq2']
    onc = sh['onc']; onr = sh['onr']; half_sb = sh.get('half_sb')
    st = contextlib.ExitStack()

    x2T = dr["x2T"]
    wqT = dr["wqT"]
    wkT = dr["wkT"]
    wvT = dr["wvT"]
    mem = dr["mem"]
    mnorm = dr["mnorm"]
    mweight = dr["mweight"]
    ones_col = dr["ones_col"]
    ones_row = dr["ones_row"]
    x3T = dr["x3T"]
    q_s = dr["qc_s"]
    q32_s = dr["qc32_s"]
    ad_s = dr["adc_s"]

    mem_sb = st.enter_context(nc.sbuf_tensor("c_mem_sb", [128, 4 * 512], BF16))
    mn_sb = st.enter_context(nc.sbuf_tensor("c_mn_sb", [128, 4], BF16))
    w_sb = st.enter_context(nc.sbuf_tensor("c_w_sb", [1, 2], F32))
    k_sb = st.enter_context(nc.sbuf_tensor("c_k_sb", [128, 4 * T], BF16))
    v_sb = st.enter_context(nc.sbuf_tensor("c_v_sb", [128, 16 * 520], BF16))
    srow = st.enter_context(nc.sbuf_tensor("c_srow", [1, 1024], F32))
    adst = st.enter_context(nc.sbuf_tensor("c_adst", [64, 1024], F32))


    with nc.Block() as blk:
        @blk.sync
        def _(sy):
            S['mi0'].dinc(sy.dma_start(onc[:, :], ones_col))
            S['mi0'].dinc(sy.dma_start(onr[:, :], ones_row))
            S['mi0'].dinc(sy.dma_start(w_sb[0:1, 0:1], mweight))
            sy.wait_ge(S['mi0'].h, S['mi0'].n)
        @blk.gpsimd
        def _(g):
            S['mi0'].dinc(g.dma_start(mn_sb[:, :], mnorm))
            S['mi0'].dinc(g.dma_start(
                mem_sb[:, :].rearrange("p (c m) -> p c m", m=512),
                mem.rearrange("(c p) m -> p c m", p=128)))
            g.wait_ge(S['mi0'].h, S['mi0'].n)
        @blk.vector
        def _(v):
            v.memset(v_sb[:, :], 1.0)
    nc.all_engine_barrier()
    with nc.Block() as blk:
        @blk.scalar
        def _(sc):
            S['prow'].inc(sc.activation(w_sb[0:1, 0:1], w_sb[0:1, 0:1], AF.Sigmoid))
        @blk.vector
        def _(v):
            v.wait_ge(S['prow'].h, S['prow'].n)
            S['prow'].inc(v.tensor_scalar(w_sb[0:1, 1:2], w_sb[0:1, 0:1], -1.0, 1.0,
                                          ALU.mult, ALU.add))
    nc.all_engine_barrier()

    c2 = contextlib.ExitStack()
    xh_sb = c2.enter_context(nc.sbuf_tensor("c_xh_sb", [128, NK * T], BF16))
    c2a = contextlib.ExitStack()
    row_m = c2a.enter_context(nc.sbuf_tensor("c_row_m", [1, T], F32))
    row_r1 = c2a.enter_context(nc.sbuf_tensor("c_row_r1", [1, T], F32))
    row_rho = c2a.enter_context(nc.sbuf_tensor("c_row_rho", [1, T], F32))
    row_mrho = c2a.enter_context(nc.sbuf_tensor("c_row_mrho", [1, T], F32))
    rows = {'m': row_m[0:1, :], 'r1': row_r1[0:1, :], 'rho': row_rho[0:1, :],
            'mrho': row_mrho[0:1, :], 't0': sqb[0:1, 0:T], 't1': sq2[0:1, 0:T]}
    bc0 = c2a.enter_context(nc.sbuf_tensor("c_bc0", [128, T], F32))
    bc1 = c2a.enter_context(nc.sbuf_tensor("c_bc1", [128, T], F32))
    emit_ln_stats(nc, S, x2T, xb, sqb, rows, onc, single=True, tag='p')
    emit_bcast(nc, S, [(rows['rho'], bc0), (rows['mrho'], bc1)], onr, tag='p')
    emit_ln_apply(nc, S, x2T, xb, bc0, bc1, xh_sb, None, None, tag='p')
    c2a.close()

    wbuf = [c2.enter_context(nc.sbuf_tensor(f"c_wbuf{i}", [128, NK * 128], BF16))
            for i in range(2)]
    wv_sb = c2.enter_context(nc.sbuf_tensor("c_wv_sb", [128, 16 * 512], BF16))
    qsl = [c2.enter_context(nc.sbuf_tensor(f"c_qsl{i}", [128, T], BF16)) for i in range(2)]
    q32sl = xb
    qs_t = q_s.rearrange("(k p) t -> k p t", p=128)
    q32_t = q32_s.rearrange("(k p) t -> k p t", p=128)

    b = {k: S[k].n for k in ('w0', 'w1', 'mmt', 'evt')}
    with nc.psum_tensor("c_psV0", [128, 512], F32) as psV0, \
         nc.psum_tensor("c_psV1", [128, 512], F32) as psV1:
        psV = [psV0, psV1]
        with nc.Block() as blk:
            @blk.gpsimd
            def _(g):
                S['w0'].dinc(g.dma_start(
                    wv_sb[:, :].rearrange("p (k c) -> p k c", c=512),
                    wvT.rearrange("(k p) c -> p k c", p=128)))
            @blk.tensor
            def _(t):
                t.wait_ge(S['pap'].h, S['pap'].n)
                t.wait_ge(S['w0'].h, b['w0'] + 16)
                for m in range(16):
                    if m >= 2:
                        t.wait_ge(S['evt'].h, b['evt'] + m - 1)
                    for k in range(NK):
                        inst = t.matmul(psV[m % 2][:, :],
                                        xh_sb[:, k * T + m * 128:k * T + (m + 1) * 128],
                                        wv_sb[:, k * 512:(k + 1) * 512],
                                        start=(k == 0), stop=(k == NK - 1))
                    S['mmt'].inc(inst)
            @blk.vector
            def _(v):
                for m in range(16):
                    v.wait_ge(S['mmt'].h, b['mmt'] + m + 1)
                    S['evt'].inc(v.tensor_copy(
                        v_sb[:, m * 520:(m + 1) * 520].rearrange(
                            "p (h c) -> p h c", c=65)[:, :, 0:64],
                        psV[m % 2][:, :].rearrange("p (h c) -> p h c", c=64)))
    nc.all_engine_barrier()

    b = {k: S[k].n for k in ('w0', 'w1', 'mmt', 'evt', 'o0', 'o1')}
    with nc.psum_tensor("c_psK0", [128, T], F32) as psK0, \
         nc.psum_tensor("c_psK1", [128, T], F32) as psK1:
        psK = [psK0, psK1]
        with nc.Block() as blk:
            @blk.gpsimd
            def _(g):
                for mi in range(20):
                    if mi >= 2:
                        g.wait_ge(S['mmt'].h, b['mmt'] + mi - 1)
                    src = wkT[mi] if mi < 4 else wqT[mi - 4]
                    S[f'w{mi % 2}'].dinc(g.dma_start(
                        wbuf[mi % 2][:, :].rearrange("p (k c) -> p k c", c=128),
                        src.rearrange("(k p) c -> p k c", p=128)))
            @blk.tensor
            def _(t):
                t.wait_ge(S['pap'].h, S['pap'].n)
                for mi in range(20):
                    t.wait_ge(S[f'w{mi % 2}'].h, b[f'w{mi % 2}'] + 16 * (mi // 2 + 1))
                    if mi >= 2:
                        t.wait_ge(S['evt'].h, b['evt'] + mi - 1)
                    for n in range(NN):
                        for k in range(NK):
                            inst = t.matmul(
                                psK[mi % 2][:, n * 512:(n + 1) * 512],
                                wbuf[mi % 2][:, k * 128:(k + 1) * 128],
                                xh_sb[:, k * T + n * 512:k * T + (n + 1) * 512],
                                start=(k == 0), stop=(k == NK - 1))
                    S['mmt'].inc(inst)
            @blk.scalar
            def _(sc):
                nq = 0
                for mi in range(20):
                    sc.wait_ge(S['mmt'].h, b['mmt'] + mi + 1)
                    if mi < 4:
                        S['evt'].inc(sc.activation(k_sb[:, mi * T:(mi + 1) * T],
                                                   psK[mi % 2][:, :], AF.Identity))
                    else:
                        if nq >= 2:
                            sc.wait_ge(S[f'o{nq % 2}'].h, b[f'o{nq % 2}'] + 32 * (nq // 2))
                        sc.activation(q32sl[nq % 2][:, :], psK[mi % 2][:, :], AF.Identity)
                        S['evt'].inc(sc.activation(qsl[nq % 2][:, :], psK[mi % 2][:, :],
                                                   AF.Identity))
                        nq += 1
            @blk.sync
            def _(sy):
                for nq in range(16):
                    mi = nq + 4
                    sy.wait_ge(S['evt'].h, b['evt'] + mi + 1)
                    S[f'o{nq % 2}'].dinc(sy.dma_start(qs_t[nq], qsl[nq % 2][:, :]))
                    S[f'o{nq % 2}'].dinc(sy.dma_start(q32_t[nq], q32sl[nq % 2][:, :]))
                for s2 in ('o0', 'o1'):
                    sy.wait_ge(S[s2].h, S[s2].n)
    nc.all_engine_barrier()
    c2.close()

    c3 = contextlib.ExitStack()
    qh = [c3.enter_context(nc.sbuf_tensor(f"c_qh{i}", [128, T], BF16)) for i in range(2)]
    etA = [c3.enter_context(nc.sbuf_tensor(f"c_etA{i}", [128, 512], BF16)) for i in range(2)]
    etB = [c3.enter_context(nc.sbuf_tensor(f"c_etB{i}", [128, 512], BF16)) for i in range(2)]
    srA = c3.enter_context(nc.sbuf_tensor("c_srA", [1, 512], F32))
    srB = c3.enter_context(nc.sbuf_tensor("c_srB", [1, 512], F32))
    adA = c3.enter_context(nc.sbuf_tensor("c_adA", [64, 512], F32))
    adB = c3.enter_context(nc.sbuf_tensor("c_adB", [64, 512], F32))
    ad_t = ad_s.rearrange("(g p) t -> g p t", p=64)
    NCH = 64
    b = {k: S[k].n for k in ('qh0', 'qh1', 'mmt', 'mmu', 'exp', 'evu',
                             'avm', 'avb', 'adv', 'advb', 'ads', 'adsb',
                             'evt', 'evtb', 'prow', 'prowb', 'pbc', 'pbcb')}
    with nc.psum_tensor("c_scA0", [128, 512], F32) as scA0, \
         nc.psum_tensor("c_scA1", [128, 512], F32) as scA1, \
         nc.psum_tensor("c_scB0", [128, 512], F32) as scB0, \
         nc.psum_tensor("c_scB1", [128, 512], F32) as scB1, \
         nc.psum_tensor("c_avA", [65, 512], F32) as avA, \
         nc.psum_tensor("c_avB", [65, 512], F32) as avB:
        scA = [scA0, scA1]; scB = [scB0, scB1]
        with nc.Block() as blk:
            @blk.gpsimd
            def _(g):
                for gp in range(16):
                    if gp >= 2:
                        g.wait_ge(S['mmt'].h, b['mmt'] + 64 * (gp - 1))
                    S[f'qh{gp % 2}'].dinc(g.dma_start(
                        qh[gp % 2][0:64, :],
                        q_s.rearrange("(g p) t -> g p t", p=64)[2 * gp]))
                    S[f'qh{gp % 2}'].dinc(g.dma_start(
                        qh[gp % 2][64:128, :],
                        q_s.rearrange("(g p) t -> g p t", p=64)[2 * gp + 1]))
            @blk.tensor
            def _(t):
                t.wait_ge(S['evt'].h, S['evt'].n - 0 if False else b['evt'])
                for gp in range(16):
                    gA = 2 * gp
                    qq, hA = divmod(gA, 8)
                    kch = hA // 2
                    t.wait_ge(S[f'qh{gp % 2}'].h, b[f'qh{gp % 2}'] + 32 * (gp // 2 + 1))
                    for sC in range(4):
                        it = gp * 4 + sC
                        soff = sC * 512
                        if it >= 1:
                            t.wait_ge(S['adv'].h, b['adv'] + it)
                            t.wait_ge(S['advb'].h, b['advb'] + it)
                        for tt in range(17):
                            g_idx = it * 16 + tt
                            if tt < 16:
                                if g_idx >= 2:
                                    t.wait_ge(S['exp'].h, b['exp'] + g_idx - 1)
                                    t.wait_ge(S['evu'].h, b['evu'] + g_idx - 1)
                                inst = t.matmul(
                                    scA[tt % 2][:, :],
                                    k_sb[:, kch * T + tt * 128:kch * T + (tt + 1) * 128][0:64, :],
                                    qh[gp % 2][0:64, soff:soff + 512],
                                    start=True, stop=True)
                                S['mmt'].inc(inst)
                                inst = t.matmul(
                                    scB[tt % 2][:, :],
                                    k_sb[:, kch * T + tt * 128:kch * T + (tt + 1) * 128][64:128, :],
                                    qh[gp % 2][64:128, soff:soff + 512],
                                    start=True, stop=True, tile_position=(64, 0))
                                S['mmu'].inc(inst)
                            if tt >= 1:
                                pt = tt - 1
                                t.wait_ge(S['exp'].h, b['exp'] + it * 16 + pt + 1)
                                inst = t.matmul(
                                    avA[:, :],
                                    v_sb[:, pt * 520 + hA * 65:pt * 520 + (hA + 1) * 65],
                                    etA[pt % 2][:, :], start=(pt == 0), stop=(pt == 15))
                                S['avm'].inc(inst)
                                t.wait_ge(S['evu'].h, b['evu'] + it * 16 + pt + 1)
                                inst = t.matmul(
                                    avB[:, :],
                                    v_sb[:, pt * 520 + (hA + 1) * 65:pt * 520 + (hA + 2) * 65],
                                    etB[pt % 2][:, :], start=(pt == 0), stop=(pt == 15))
                                S['avb'].inc(inst)
                        t.wait_ge(S['prow'].h, b['prow'] + it + 1)
                        t.wait_ge(S['evt'].h, b['evt'] + it + 1)
                        S['pbc'].inc(t.matmul(avA[0:64, :], onr[0:1, 0:64],
                                              srA[0:1, :], start=True, stop=True))
                        t.wait_ge(S['prowb'].h, b['prowb'] + it + 1)
                        t.wait_ge(S['evtb'].h, b['evtb'] + it + 1)
                        S['pbcb'].inc(t.matmul(avB[0:64, :], onr[0:1, 0:64],
                                               srB[0:1, :], start=True, stop=True))
            @blk.scalar
            def _(sc):
                for it in range(NCH):
                    for tt in range(16):
                        g_idx = it * 16 + tt
                        sc.wait_ge(S['mmt'].h, b['mmt'] + g_idx + 1)
                        if g_idx >= 2:
                            sc.wait_ge(S['avm'].h, b['avm'] + g_idx - 1)
                        S['exp'].inc(sc.activation(etA[tt % 2][:, :], scA[tt % 2][:, :],
                                                   AF.Exp, scale=0.125))
                        sc.wait_ge(S['mmu'].h, b['mmu'] + g_idx + 1)
                        if g_idx >= 2:
                            sc.wait_ge(S['avb'].h, b['avb'] + g_idx - 1)
                        S['evu'].inc(sc.activation(etB[tt % 2][:, :], scB[tt % 2][:, :],
                                                   AF.Exp, scale=0.125))
                    sc.wait_ge(S['avm'].h, b['avm'] + (it + 1) * 16)
                    if it >= 1:
                        sc.wait_ge(S['ads'].h, b['ads'] + 16 * it)
                    sc.activation(adA[:, :], avA[0:64, :], AF.Identity)
                    sc.activation(srA[0:1, :], avA[64:65, :], AF.Ln)
                    S['evt'].inc(sc.activation(srA[0:1, :], srA[0:1, :],
                                               AF.Exp, scale=-1.0))
                    sc.wait_ge(S['avb'].h, b['avb'] + (it + 1) * 16)
                    if it >= 1:
                        sc.wait_ge(S['adsb'].h, b['adsb'] + 16 * it)
                    sc.activation(adB[:, :], avB[0:64, :], AF.Identity)
                    sc.activation(srB[0:1, :], avB[64:65, :], AF.Ln)
                    S['evtb'].inc(sc.activation(srB[0:1, :], srB[0:1, :],
                                                AF.Exp, scale=-1.0))
            @blk.vector
            def _(v):
                for it in range(NCH):
                    v.wait_ge(S['evt'].h, b['evt'] + it + 1)
                    S['prow'].inc(v.tensor_single_scalar(srA[0:1, :], srA[0:1, :],
                                                         w_sb[0:1, 1:2], ALU.mult))
                    v.wait_ge(S['evtb'].h, b['evtb'] + it + 1)
                    S['prowb'].inc(v.tensor_single_scalar(srB[0:1, :], srB[0:1, :],
                                                          w_sb[0:1, 1:2], ALU.mult))
                    v.wait_ge(S['pbc'].h, b['pbc'] + it + 1)
                    S['adv'].inc(v.tensor_tensor(adA[:, :], adA[:, :],
                                                 avA[0:64, :], ALU.mult))
                    v.wait_ge(S['pbcb'].h, b['pbcb'] + it + 1)
                    S['advb'].inc(v.tensor_tensor(adB[:, :], adB[:, :],
                                                  avB[0:64, :], ALU.mult))
            @blk.sync
            def _(sy):
                for it in range(NCH):
                    gp, sC = divmod(it, 4)
                    sy.wait_ge(S['adv'].h, b['adv'] + it + 1)
                    S['ads'].dinc(sy.dma_start(
                        ad_t[2 * gp, :, sC * 512:(sC + 1) * 512], adA[:, :]))
                    sy.wait_ge(S['advb'].h, b['advb'] + it + 1)
                    S['adsb'].dinc(sy.dma_start(
                        ad_t[2 * gp + 1, :, sC * 512:(sC + 1) * 512], adB[:, :]))
                sy.wait_ge(S['ads'].h, S['ads'].n)
                sy.wait_ge(S['adsb'].h, S['adsb'].n)
    nc.all_engine_barrier()
    c3.close()

    c4 = contextlib.ExitStack()
    sg_sb = c4.enter_context(nc.sbuf_tensor("c_sg_sb", [128, 4 * T], BF16))
    bcw = c4.enter_context(nc.sbuf_tensor("c_bcw", [128, T], F32))
    adb = [c4.enter_context(nc.sbuf_tensor(f"c_adb{i}", [128, T], F32)) for i in range(2)]
    rows_t0 = sqb[0:1, 0:T]
    x2_t = x2T.rearrange("(k p) t -> k p t", p=128)
    x3_t = x3T.rearrange("(k p) t -> k p t", p=128)
    adf_t = ad_s.rearrange("(k p) t -> k p t", p=128)
    for qq in range(4):
        b = {k: S[k].n for k in ('px0', 'px1', 'psq', 'dv')}
        with nc.psum_tensor(f"c_dps{qq}", [1, T], F32) as dps:
            with nc.Block() as blk:
                @blk.sync
                def _(sy):
                    for c in range(4):
                        if c >= 2:
                            sy.wait_ge(S['dv'].h, b['dv'] + 2 * (c - 1))
                        S[f'px{c % 2}'].dinc(sy.dma_start(xb[c % 2][:, :],
                                                          q32_t[qq * 4 + c]))
                @blk.vector
                def _(v):
                    for c in range(4):
                        v.wait_ge(S[f'px{c % 2}'].h, b[f'px{c % 2}'] + 16 * (c // 2 + 1))
                        v.tensor_scalar_min(sqb[:, :], xb[c % 2][:, :], 0.0)
                        S['dv'].inc(v.tensor_scalar_max(sq2[:, :], xb[c % 2][:, :], 0.0))
                        v.wait_ge(S['psq'].h, b['psq'] + c + 1)
                        S['dv'].inc(v.scalar_tensor_tensor(
                            sg_sb[:, c * T:(c + 1) * T], bcw[:, :], -1.0, sq2[:, :],
                            ALU.add, ALU.add))
                @blk.scalar
                def _(sc):
                    for c in range(4):
                        sc.wait_ge(S['dv'].h, b['dv'] + 2 * c + 1)
                        S['psq'].inc(sc.activation(bcw[:, :], sqb[:, :], AF.Exp))
            nc.all_engine_barrier()
            b2 = {k: S[k].n for k in ('mmt', 'prow')}
            with nc.Block() as blk:
                @blk.tensor
                def _(t):
                    t.wait_ge(S['dv'].h, S['dv'].n)
                    for n in range(NN):
                        for kc in range(4):
                            inst = t.matmul(dps[:, n * 512:(n + 1) * 512],
                                            mn_sb[:, kc:kc + 1],
                                            sg_sb[:, kc * T + n * 512:kc * T + (n + 1) * 512],
                                            start=(kc == 0), stop=(kc == 3))
                    S['mmt'].inc(inst)
                @blk.vector
                def _(v):
                    v.wait_ge(S['mmt'].h, b2['mmt'] + 1)
                    v.reciprocal(rows_t0, dps[:, :])
                    S['prow'].inc(v.tensor_single_scalar(rows_t0, rows_t0,
                                                         w_sb[0:1, 0:1], ALU.mult))
            nc.all_engine_barrier()
        emit_bcast(nc, S, [(rows_t0, bcw)], onr, tag='p')
        b3 = {k: S[k].n for k in ('mmt', 'dv', 'o0', 'o1', 'px0', 'px1',
                                  'qh0', 'qh1')}
        with nc.psum_tensor(f"c_amps0_{qq}", [128, T], F32) as amps0, \
             nc.psum_tensor(f"c_amps1_{qq}", [128, T], F32) as amps1:
            amps = [amps0, amps1]
            with nc.Block() as blk:
                @blk.tensor
                def _(t):
                    t.wait_ge(S['dv'].h, S['dv'].n)
                    t.wait_ge(S['pbce'].h, S['pbce'].n)
                    for mt in range(4):
                        if mt >= 2:
                            t.wait_ge(S['dv'].h, b3['dv'] + mt - 1)
                        for n in range(NN):
                            for kc in range(4):
                                inst = t.matmul(
                                    amps[mt % 2][:, n * 512:(n + 1) * 512],
                                    mem_sb[:, kc * 512 + mt * 128:kc * 512 + (mt + 1) * 128],
                                    sg_sb[:, kc * T + n * 512:kc * T + (n + 1) * 512],
                                    start=(kc == 0), stop=(kc == 3))
                        S['mmt'].inc(inst)
                @blk.sync
                def _(sy):
                    for mt in range(2):
                        S[f'px{mt % 2}'].dinc(sy.dma_start(adb[mt % 2][:, :],
                                                           adf_t[qq * 4 + mt]))
                        S[f'qh{mt % 2}'].dinc(sy.dma_start(xb[mt % 2][:, :],
                                                           x2_t[qq * 4 + mt]))
                    for mt in range(4):
                        sy.wait_ge(S['dv'].h, b3['dv'] + mt + 1)
                        S[f'o{mt % 2}'].dinc(sy.dma_start(x3_t[qq * 4 + mt],
                                                          adb[mt % 2][:, :]))
                        if mt + 2 < 4:
                            S[f'px{mt % 2}'].dinc(sy.dma_start(
                                adb[mt % 2][:, :], adf_t[qq * 4 + mt + 2]))
                            S[f'qh{mt % 2}'].dinc(sy.dma_start(
                                xb[mt % 2][:, :], x2_t[qq * 4 + mt + 2]))
                    for s2 in ('o0', 'o1'):
                        sy.wait_ge(S[s2].h, S[s2].n)
                @blk.vector
                def _(v):
                    for mt in range(4):
                        v.wait_ge(S['mmt'].h, b3['mmt'] + mt + 1)
                        v.wait_ge(S[f'px{mt % 2}'].h, b3[f'px{mt % 2}'] + 16 * (mt // 2 + 1))
                        v.wait_ge(S[f'qh{mt % 2}'].h, b3[f'qh{mt % 2}'] + 16 * (mt // 2 + 1))
                        v.tensor_tensor(sq2[:, :], amps[mt % 2][:, :], bcw[:, :], ALU.mult)
                        v.tensor_add(sq2[:, :], sq2[:, :], xb[mt % 2][:, :])
                        S['dv'].inc(v.tensor_add(adb[mt % 2][:, :], adb[mt % 2][:, :],
                                                 sq2[:, :]))
        nc.all_engine_barrier()
    c4.close()
    st.close()
    nc.all_engine_barrier()



def emit_sec_D(nc, S, sh, dr, debug=False):
    xb = sh['xb']; sqb = sh['sqb']; sq2 = sh['sq2']
    onc = sh['onc']; onr = sh['onr']; half_sb = sh.get('half_sb')
    st = contextlib.ExitStack()
    TQ = 512
    NQ = 4

    x3T = dr["x3T"]
    gwT = dr["gwT"]
    swg = dr["swg"]
    swu = dr["swu"]
    swd = dr["swd"]
    eg = dr["eg"]
    eu = dr["eu"]
    ed = dr["ed"]
    ones_col = dr["ones_col"]
    ones_row = dr["ones_row"]
    idt = dr["idt"]
    outT = dr["outT"]
    werow_s = dr["werow_s"]
    if debug:
        dbg_we = dr["dbg_we"]
        dbg_g = dr["dbg_g"]
        dbg_wsb = dr["dbg_wsb"]

    row_m = st.enter_context(nc.sbuf_tensor("d_row_m", [1, T], F32))
    row_r1 = st.enter_context(nc.sbuf_tensor("d_row_r1", [1, T], F32))
    row_rho = st.enter_context(nc.sbuf_tensor("d_row_rho", [1, T], F32))
    row_mrho = st.enter_context(nc.sbuf_tensor("d_row_mrho", [1, T], F32))
    rows = {'m': row_m[0:1, :], 'r1': row_r1[0:1, :], 'rho': row_rho[0:1, :],
            'mrho': row_mrho[0:1, :], 't0': sqb[0:1, 0:T], 't1': sq2[0:1, 0:T]}
    bc0 = st.enter_context(nc.sbuf_tensor("d_bc0", [128, T], F32))
    bc1 = st.enter_context(nc.sbuf_tensor("d_bc1", [128, T], F32))
    idt_sb = st.enter_context(nc.sbuf_tensor("d_idt_sb", [128, 128], F32))
    gw_sb = st.enter_context(nc.sbuf_tensor("d_gw_sb", [128, 16 * 8], BF16))
    xn_sb = st.enter_context(nc.sbuf_tensor("d_xn_sb", [128, NK * TQ], BF16))
    acc = st.enter_context(nc.sbuf_tensor("d_acc", [128, 16 * TQ], F32))
    hs_sb = st.enter_context(nc.sbuf_tensor("d_hs_sb", [128, 22 * TQ], BF16))
    wgb = [st.enter_context(nc.sbuf_tensor(f"d_wgb{i}", [128, D], BF16)) for i in range(2)]
    wub = [st.enter_context(nc.sbuf_tensor(f"d_wub{i}", [128, D], BF16)) for i in range(2)]
    wdb = [st.enter_context(nc.sbuf_tensor(f"d_wdb{i}", [128, 2816], BF16)) for i in range(2)]
    sil = [st.enter_context(nc.sbuf_tensor(f"d_sil{i}", [128, TQ], BF16)) for i in range(2)]
    g_sb = st.enter_context(nc.sbuf_tensor("d_g_sb", [128, 32], F32))
    we_sb = st.enter_context(nc.sbuf_tensor("d_we_sb", [128, 32], F32))
    m14 = st.enter_context(nc.sbuf_tensor("d_m14", [128, 4], F32))
    m24 = st.enter_context(nc.sbuf_tensor("d_m24", [128, 4], F32))
    ms4 = st.enter_context(nc.sbuf_tensor("d_ms4", [128, 4], F32))
    rs4 = st.enter_context(nc.sbuf_tensor("d_rs4", [128, 4], F32))
    negm = st.enter_context(nc.sbuf_tensor("d_negm", [128, 32], F32))
    mkb = st.enter_context(nc.sbuf_tensor("d_mkb", [128, 32], F32))
    mk2 = st.enter_context(nc.sbuf_tensor("d_mk2", [128, 32], F32))
    wtmp = st.enter_context(nc.sbuf_tensor("d_wtmp", [1, TQ], F32))
    bcwe = st.enter_context(nc.sbuf_tensor("d_bcwe", [128, TQ], F32))


    with nc.Block() as blk:
        @blk.sync
        def _(sy):
            S['mi0'].dinc(sy.dma_start(onc[:, :], ones_col))
            S['mi0'].dinc(sy.dma_start(onr[:, :], ones_row))
            S['mi0'].dinc(sy.dma_start(idt_sb[:, :], idt))
            sy.wait_ge(S['mi0'].h, S['mi0'].n)
        @blk.gpsimd
        def _(g):
            S['mi0'].dinc(g.dma_start(
                gw_sb[:, :].rearrange("p (k c) -> p k c", c=8),
                gwT.rearrange("(k p) c -> p k c", p=128)))
            g.wait_ge(S['mi0'].h, S['mi0'].n)
    nc.all_engine_barrier()

    emit_ln_stats(nc, S, x3T, xb, sqb, rows, onc, single=True, tag='p')
    emit_bcast(nc, S, [(rows['rho'], bc0), (rows['mrho'], bc1)], onr, tag='p')

    x3_t = x3T.rearrange("(k p) t -> k p t", p=128)
    out_t = outT.rearrange("(k p) t -> k p t", p=128)

    for Q in range(NQ):
        off = Q * TQ
        emit_ln_apply(nc, S, x3T, xb, bc0, bc1, xn_sb, None, None, tag='p',
                      half=None, seg=(off, TQ))

        b = {k: S[k].n for k in ('mmt', 'evt', 'dv', 'gp', 'prow', 'tp',
                                 'pbc', 'wr0', 'psq')}
        with nc.psum_tensor(f"d_gps{Q}", [128, 8], F32) as gps, \
             nc.psum_tensor(f"d_tpps{Q}", [1, 128], F32) as tpps:
            with nc.Block() as blk:
                @blk.tensor
                def _(t):
                    t.wait_ge(S['pap'].h, S['pap'].n)
                    for tt in range(4):
                        if tt >= 1:
                            t.wait_ge(S['evt'].h, b['evt'] + tt)
                        for k in range(NK):
                            inst = t.matmul(gps[:, :],
                                            xn_sb[:, k * TQ + tt * 128:k * TQ + (tt + 1) * 128],
                                            gw_sb[:, k * 8:(k + 1) * 8],
                                            start=(k == 0), stop=(k == NK - 1))
                        S['mmt'].inc(inst)
                    for e in range(8):
                        for tt in range(4):
                            t.wait_ge(S['gp'].h, b['gp'] + 16)
                            if e + tt > 0:
                                t.wait_ge(S['prow'].h, b['prow'] + 4 * e + tt)
                            S['tp'].inc(t.transpose(tpps[:, :],
                                                    we_sb[:, tt * 8 + e:tt * 8 + e + 1],
                                                    idt_sb[:, :]))
                @blk.scalar
                def _(sc):
                    for tt in range(4):
                        sc.wait_ge(S['mmt'].h, b['mmt'] + tt + 1)
                        S['evt'].inc(sc.activation(g_sb[:, tt * 8:(tt + 1) * 8],
                                                   gps[:, :], AF.Sigmoid))
                    sc.wait_ge(S['dv'].h, b['dv'] + 12)
                    for tt in range(4):
                        S['psq'].inc(sc.activation(ms4[:, tt:tt + 1],
                                                   m24[:, tt:tt + 1],
                                                   AF.Identity, bias=m14[:, tt:tt + 1]))
                @blk.vector
                def _(v):
                    for tt in range(4):
                        v.wait_ge(S['evt'].h, b['evt'] + tt + 1)
                        S['dv'].inc(v.reduce_max(m14[:, tt:tt + 1],
                                                 g_sb[:, tt * 8:(tt + 1) * 8], axis=AX))
                    v.wait_ge(S['gp'].h, b['gp'] + 4)
                    for tt in range(4):
                        S['dv'].inc(v.tensor_add(mkb[:, tt * 8:(tt + 1) * 8],
                                                 g_sb[:, tt * 8:(tt + 1) * 8],
                                                 negm[:, tt * 8:(tt + 1) * 8]))
                    v.wait_ge(S['gp'].h, b['gp'] + 8)
                    for tt in range(4):
                        S['dv'].inc(v.reduce_max(m24[:, tt:tt + 1],
                                                 mk2[:, tt * 8:(tt + 1) * 8], axis=AX))
                    v.wait_ge(S['psq'].h, b['psq'] + 4)
                    S['dv'].inc(v.reciprocal(rs4[:, :], ms4[:, :]))
                    v.wait_ge(S['gp'].h, b['gp'] + 12)
                    for tt in range(4):
                        S['dv'].inc(v.tensor_mul(mkb[:, tt * 8:(tt + 1) * 8],
                                                 g_sb[:, tt * 8:(tt + 1) * 8],
                                                 negm[:, tt * 8:(tt + 1) * 8]))
                    for e in range(8):
                        if e >= 1:
                            v.wait_ge(S['wr0'].h, b['wr0'] + 16 * e)
                        for tt in range(4):
                            v.wait_ge(S['tp'].h, b['tp'] + 4 * e + tt + 1)
                            S['prow'].inc(v.tensor_copy(
                                wtmp[0:1, tt * 128:(tt + 1) * 128], tpps[:, :]))
                @blk.gpsimd
                def _(g):
                    g.wait_ge(S['dv'].h, b['dv'] + 4)
                    for tt in range(4):
                        S['gp'].inc(g.tensor_scalar(negm[:, tt * 8:(tt + 1) * 8],
                                                    g_sb[:, tt * 8:(tt + 1) * 8],
                                                    m14[:, tt:tt + 1], -1.0e30,
                                                    ALU.is_equal, ALU.mult))
                    g.wait_ge(S['dv'].h, b['dv'] + 8)
                    for tt in range(4):
                        S['gp'].inc(g.tensor_copy(mk2[:, tt * 8:(tt + 1) * 8],
                                                  mkb[:, tt * 8:(tt + 1) * 8]))
                    g.wait_ge(S['dv'].h, b['dv'] + 12)
                    for tt in range(4):
                        S['gp'].inc(g.tensor_single_scalar(
                            negm[:, tt * 8:(tt + 1) * 8],
                            g_sb[:, tt * 8:(tt + 1) * 8], m24[:, tt:tt + 1], ALU.is_ge))
                    g.wait_ge(S['dv'].h, b['dv'] + 17)
                    for tt in range(4):
                        S['gp'].inc(g.tensor_scalar(we_sb[:, tt * 8:(tt + 1) * 8],
                                                    mkb[:, tt * 8:(tt + 1) * 8],
                                                    rs4[:, tt:tt + 1], 2.5,
                                                    ALU.mult, ALU.mult))
                @blk.sync
                def _(sy):
                    for e in range(8):
                        sy.wait_ge(S['prow'].h, b['prow'] + 4 * e + 4)
                        S['wr0'].dinc(sy.dma_start(werow_s[e:e + 1, off:off + TQ],
                                                   wtmp[0:1, :]))
                    if debug:
                        sy.wait_ge(S['gp'].h, b['gp'] + 16)
                        S['wr0'].dinc(sy.dma_start(dbg_g[Q], g_sb[:, :]))
                        S['wr0'].dinc(sy.dma_start(dbg_wsb[Q], we_sb[:, :]))
                    sy.wait_ge(S['wr0'].h, S['wr0'].n)
        nc.all_engine_barrier()

        phases = [('s', 22, 16)] + [(e, 11, 16) for e in range(8)]
        g_of = []
        d_of = []
        for pi, (ph, nmt, nm) in enumerate(phases):
            for mt in range(nmt):
                g_of.append((pi, mt))
            for m in range(nm):
                d_of.append((pi, m))
        NG = len(g_of)
        ND = len(d_of)
        hreq = []
        c = 0
        for pi, (ph, nmt, nm) in enumerate(phases):
            c += nmt
            hreq.append(c)
        b = {k: S[k].n for k in ('w0', 'w1', 'u0', 'u1', 'd0', 'd1', 'mmt',
                                 'mmu', 'mmd', 'evt', 'dvh', 'dva', 'pbc',
                                 'gp', 'wr0')}
        with nc.psum_tensor(f"d_psG0_{Q}", [128, TQ], F32) as psG0, \
             nc.psum_tensor(f"d_psG1_{Q}", [128, TQ], F32) as psG1, \
             nc.psum_tensor(f"d_psU0_{Q}", [128, TQ], F32) as psU0, \
             nc.psum_tensor(f"d_psU1_{Q}", [128, TQ], F32) as psU1, \
             nc.psum_tensor(f"d_psD0_{Q}", [128, TQ], F32) as psD0, \
             nc.psum_tensor(f"d_psD1_{Q}", [128, TQ], F32) as psD1, \
             nc.psum_tensor(f"d_psBc_{Q}", [128, TQ], F32) as psBc:
            psG = [psG0, psG1]; psU = [psU0, psU1]; psDl = [psD0, psD1]
            def g_src(pi, mt):
                ph = phases[pi][0]
                if ph == 's':
                    return swg[mt], swu[mt]
                return eg[ph, mt], eu[ph, mt]
            def d_src(pi, m):
                ph = phases[pi][0]
                return swd[m] if ph == 's' else ed[ph, m]
            with nc.Block() as blk:
                @blk.gpsimd
                def _(g):
                    j = 0
                    jj = 0
                    for pi, (ph, nmt, nm) in enumerate(phases):
                        for mt in range(nmt):
                            if j >= 2:
                                g.wait_ge(S['mmt'].h, b['mmt'] + j - 1)
                            sg, su = g_src(pi, mt)
                            S[f'w{j % 2}'].dinc(g.dma_start(
                                wgb[j % 2][:, :].rearrange("p (k c) -> p k c", c=128),
                                sg.rearrange("(k p) c -> p k c", p=128)))
                            if j >= 2:
                                g.wait_ge(S['mmu'].h, b['mmu'] + j - 1)
                            S[f'u{j % 2}'].dinc(g.dma_start(
                                wub[j % 2][:, :].rearrange("p (k c) -> p k c", c=128),
                                su.rearrange("(k p) c -> p k c", p=128)))
                            j += 1
                        for m in range(nm):
                            if jj >= 2:
                                g.wait_ge(S['mmd'].h, b['mmd'] + jj - 1)
                            sd = d_src(pi, m)
                            kw = 22 if ph == 's' else 11
                            S[f'd{jj % 2}'].dinc(g.dma_start(
                                wdb[jj % 2][:, 0:kw * 128].rearrange(
                                    "p (k c) -> p k c", c=128),
                                sd.rearrange("(k p) c -> p k c", p=128)))
                            jj += 1
                @blk.sync
                def _(sy):
                    for e in range(8):
                        sy.wait_ge(S['pbc'].h, b['pbc'] + e)
                        S['wr0'].dinc(sy.dma_start(wtmp[0:1, :],
                                                   werow_s[e:e + 1, off:off + TQ]))
                @blk.tensor
                def _(t):
                    t.wait_ge(S['pap'].h, S['pap'].n)
                    jj = 0
                    for pi, (ph, nmt, nm) in enumerate(phases):
                        if ph != 's':
                            e = ph
                            t.wait_ge(S['wr0'].h, b['wr0'] + 16 * (e + 1))
                            t.wait_ge(S['gp'].h, b['gp'] + e)
                            S['pbc'].inc(t.matmul(psBc[:, :], onr[:, :], wtmp[0:1, :],
                                                  start=True, stop=True))
                        for mt in range(nmt):
                            j = len([0 for p2 in range(pi) for _ in range(phases[p2][1])]) + mt
                            t.wait_ge(S[f'w{j % 2}'].h, b[f'w{j % 2}'] + 16 * (j // 2 + 1))
                            if j >= 2:
                                t.wait_ge(S['evt'].h, b['evt'] + j - 1)
                            for k in range(NK):
                                inst = t.matmul(psG[j % 2][:, :],
                                                wgb[j % 2][:, k * 128:(k + 1) * 128],
                                                xn_sb[:, k * TQ:(k + 1) * TQ],
                                                start=(k == 0), stop=(k == NK - 1))
                            S['mmt'].inc(inst)
                            t.wait_ge(S[f'u{j % 2}'].h, b[f'u{j % 2}'] + 16 * (j // 2 + 1))
                            if j >= 2:
                                t.wait_ge(S['dvh'].h, b['dvh'] + j - 1)
                            for k in range(NK):
                                inst = t.matmul(psU[j % 2][:, :],
                                                wub[j % 2][:, k * 128:(k + 1) * 128],
                                                xn_sb[:, k * TQ:(k + 1) * TQ],
                                                start=(k == 0), stop=(k == NK - 1))
                            S['mmu'].inc(inst)
                        for m in range(nm):
                            kw = 22 if ph == 's' else 11
                            t.wait_ge(S[f'd{jj % 2}'].h, b[f'd{jj % 2}'] + 16 * (jj // 2 + 1))
                            t.wait_ge(S['dvh'].h, b['dvh'] + hreq[pi])
                            if jj >= 2:
                                t.wait_ge(S['dva'].h, b['dva'] + jj - 1)
                            for k in range(kw):
                                inst = t.matmul(psDl[jj % 2][:, :],
                                                wdb[jj % 2][:, k * 128:(k + 1) * 128],
                                                hs_sb[:, k * TQ:(k + 1) * TQ],
                                                start=(k == 0), stop=(k == kw - 1))
                            S['mmd'].inc(inst)
                            jj += 1
                @blk.scalar
                def _(sc):
                    for j in range(NG):
                        sc.wait_ge(S['mmt'].h, b['mmt'] + j + 1)
                        if j >= 2:
                            sc.wait_ge(S['dvh'].h, b['dvh'] + j - 1)
                        S['evt'].inc(sc.activation(sil[j % 2][:, 0:TQ],
                                                   psG[j % 2][:, :], AF.Silu))
                @blk.vector
                def _(v):
                    jj = 0
                    j = 0
                    for pi, (ph, nmt, nm) in enumerate(phases):
                        if ph != 's':
                            v.wait_ge(S['pbc'].h, b['pbc'] + ph + 1)
                            S['gp'].inc(v.tensor_copy(bcwe[:, 0:TQ], psBc[:, :]))
                        for mt in range(nmt):
                            v.wait_ge(S['evt'].h, b['evt'] + j + 1)
                            v.wait_ge(S['mmu'].h, b['mmu'] + j + 1)
                            if ph == 's':
                                S['dvh'].inc(v.tensor_tensor(
                                    hs_sb[:, mt * TQ:(mt + 1) * TQ],
                                    sil[j % 2][:, 0:TQ], psU[j % 2][:, :], ALU.mult))
                            else:
                                v.tensor_tensor(sqb[:, 0:TQ], sil[j % 2][:, 0:TQ],
                                                psU[j % 2][:, :], ALU.mult)
                                S['dvh'].inc(v.tensor_tensor(
                                    hs_sb[:, mt * TQ:(mt + 1) * TQ],
                                    sqb[:, 0:TQ], bcwe[:, 0:TQ], ALU.mult))
                            j += 1
                        for m in range(nm):
                            v.wait_ge(S['mmd'].h, b['mmd'] + jj + 1)
                            if ph == 's':
                                S['dva'].inc(v.tensor_copy(acc[:, m * TQ:(m + 1) * TQ],
                                                           psDl[jj % 2][:, :]))
                            else:
                                S['dva'].inc(v.tensor_tensor(
                                    acc[:, m * TQ:(m + 1) * TQ], psDl[jj % 2][:, :],
                                    acc[:, m * TQ:(m + 1) * TQ], ALU.add))
                            jj += 1
        nc.all_engine_barrier()

        b = {k: S[k].n for k in ('px0', 'px1', 'dv', 'o0', 'o1')}
        with nc.Block() as blk:
            @blk.sync
            def _(sy):
                S['px0'].dinc(sy.dma_start(xb[0][:, 0:TQ], x3_t[0, :, off:off + TQ]))
                S['px1'].dinc(sy.dma_start(xb[1][:, 0:TQ], x3_t[1, :, off:off + TQ]))
                for m in range(16):
                    sy.wait_ge(S['dv'].h, b['dv'] + m + 1)
                    S[f'o{m % 2}'].dinc(sy.dma_start(out_t[m, :, off:off + TQ],
                                                     xb[m % 2][:, 0:TQ]))
                    if m + 2 < 16:
                        S[f'px{m % 2}'].dinc(sy.dma_start(
                            xb[m % 2][:, 0:TQ], x3_t[m + 2, :, off:off + TQ]))
                for s2 in ('o0', 'o1'):
                    sy.wait_ge(S[s2].h, S[s2].n)
            @blk.vector
            def _(v):
                for m in range(16):
                    v.wait_ge(S[f'px{m % 2}'].h, b[f'px{m % 2}'] + 16 * (m // 2 + 1))
                    if m >= 2:
                        v.wait_ge(S[f'o{m % 2}'].h, b[f'o{m % 2}'] + 16 * (m // 2))
                    S['dv'].inc(v.tensor_tensor(xb[m % 2][:, 0:TQ],
                                                acc[:, m * TQ:(m + 1) * TQ],
                                                xb[m % 2][:, 0:TQ], ALU.add))
        nc.all_engine_barrier()
    if debug:
        with nc.Block() as blk:
            @blk.sync
            def _(sy):
                S['mi0'].dinc(sy.dma_start(dbg_we, werow_s))
                sy.wait_ge(S['mi0'].h, S['mi0'].n)
        nc.all_engine_barrier()
    st.close()
    nc.all_engine_barrier()



def perm_pi():
    return np.concatenate([np.arange(0, 1024, 2), np.arange(1, 1024, 2),
                           np.arange(1024, 2048)])




def prep_A(X, W_in, b_in, W_out, b_out, gamma, beta):
    pi = perm_pi()
    colperm = np.concatenate([pi, 2048 + pi, 4096 + pi])
    W_in_p = np.ascontiguousarray(W_in[:, colperm])
    b_in_p = np.ascontiguousarray(b_in[colperm])
    W_out_p = np.ascontiguousarray(W_out[pi, :])
    w_in_blocks = np.ascontiguousarray(
        W_in_p.reshape(2048, 48, 128).transpose(1, 0, 2))
    b_in_t = np.ascontiguousarray(b_in_p.reshape(48, 128).T)
    b_out_t = np.ascontiguousarray(b_out.reshape(16, 128).T)
    gamma_t = np.ascontiguousarray(gamma.reshape(16, 128).T)
    beta_t = np.ascontiguousarray(beta.reshape(16, 128).T)
    inv_freq = (1.0 / 10000.0 ** (np.arange(0, 1024, 2).astype(np.float64) / 1024.0))
    freqs = np.arange(T, dtype=np.float64)[None, :] * inv_freq[:, None]
    cosT = np.cos(freqs).astype(np.float32)
    sinT = np.sin(freqs).astype(np.float32)
    shared = {
        "w_in": w_in_blocks.astype(np.float32), "b_in": b_in_t.astype(np.float32),
        "w_out": W_out_p.astype(np.float32), "b_out": b_out_t.astype(np.float32),
        "gamma": gamma_t.astype(np.float32), "beta": beta_t.astype(np.float32),
        "cosT": cosT, "sinT": sinT,
        "ones_col": np.ones((128, 1), np.float32),
        "ones_row": np.ones((1, 128), np.float32),
    }
    maps = []
    for bb in range(8):
        m = dict(shared)
        m["xT"] = np.ascontiguousarray(X[bb].T)
        maps.append(m)
    return maps





def prep_B(W, ln_w, ln_b):
    shared = {
        "w": np.ascontiguousarray(W.astype(np.float32)),
        "lnw": np.ascontiguousarray(ln_w.reshape(16, 128).T).astype(np.float32),
        "lnb": np.ascontiguousarray(ln_b.reshape(16, 128).T).astype(np.float32),
        "ones_col": np.ones((128, 1), np.float32),
        "ones_row": np.ones((1, 128), np.float32),
    }
    return shared





def prep_C(Wq, Wk, Wv, memory, memory_norm, memory_weight):
    wqT = np.ascontiguousarray(Wq.T)
    wkT = np.ascontiguousarray(Wk.T)
    wvT = np.ascontiguousarray(Wv.T)
    return {
        "wqT": np.ascontiguousarray(wqT.reshape(2048, 16, 128).transpose(1, 0, 2)),
        "wkT": np.ascontiguousarray(wkT.reshape(2048, 4, 128).transpose(1, 0, 2)),
        "wvT": wvT.astype(np.float32),
        "mem": memory.astype(np.float32),
        "mnorm": np.ascontiguousarray(memory_norm.reshape(4, 128).T).astype(np.float32),
        "mweight": memory_weight.reshape(1, 1).astype(np.float32),
        "ones_col": np.ones((128, 1), np.float32),
        "ones_row": np.ones((1, 128), np.float32),
    }





def prep_D(gate_w, shared_wg, shared_wu, shared_wd, exp_wg, exp_wu, exp_wd):
    def blocks(WT, nb):
        K, M = WT.shape
        return np.ascontiguousarray(WT.reshape(K, nb, 128).transpose(1, 0, 2))
    swgT = np.ascontiguousarray(shared_wg.T)
    swuT = np.ascontiguousarray(shared_wu.T)
    swdT = np.ascontiguousarray(shared_wd.T)
    egT = np.stack([blocks(np.ascontiguousarray(exp_wg[e].T), 11) for e in range(8)])
    euT = np.stack([blocks(np.ascontiguousarray(exp_wu[e].T), 11) for e in range(8)])
    edT = np.stack([blocks(np.ascontiguousarray(exp_wd[e].T), 16) for e in range(8)])
    return {
        "gwT": np.ascontiguousarray(gate_w.T).astype(np.float32),
        "swg": blocks(swgT, 22).astype(np.float32),
        "swu": blocks(swuT, 22).astype(np.float32),
        "swd": blocks(swdT, 16).astype(np.float32),
        "eg": egT.astype(np.float32), "eu": euT.astype(np.float32),
        "ed": edT.astype(np.float32),
        "ones_col": np.ones((128, 1), np.float32),
        "ones_row": np.ones((1, 128), np.float32),
        "idt": np.eye(128, dtype=np.float32),
    }




ALL_SEMS = ['px0', 'px1', 'psq', 'pmm', 'prow', 'pap', 'pbc', 'pbce',
            'w0', 'w1', 'u0', 'u1', 'd0', 'd1', 'mmt', 'mmu', 'evt', 'evu',
            'dv', 'gp', 'mi0', 'o0', 'o1', 'q0', 'q1', 'k0', 'r0', 'r1s',
            'cs0', 'rot', 'qh0', 'qh1', 'exp', 'avm', 'adv', 'ads', 'tp', 'wr0', 'qz', 'mmd', 'dvh', 'dva', 'avb', 'advb', 'adsb', 'evtb', 'prowb', 'pbcb']


def build_full(debug=False):
    nc = bass.Bass("TRN2", target_bir_lowering=False)
    cleanup = nc.cleanup_on_exit()
    cleanup.__enter__()
    top = contextlib.ExitStack()

    dr = {}
    def din(name, shape):
        dr[name] = nc.dram_tensor(name, shape, F32, kind="ExternalInput").ap()
    din("xT", [D, T]); din("w_in", [48, D, 128]); din("b_in", [128, 48])
    din("w_out", [D, D]); din("b_out", [128, 16]); din("gamma", [128, 16])
    din("beta", [128, 16]); din("cosT", [512, T]); din("sinT", [512, T])
    din("ones_col", [128, 1]); din("ones_row", [1, 128])
    din("w", [D, D]); din("lnw", [128, 16]); din("lnb", [128, 16])
    din("wqT", [16, D, 128]); din("wkT", [4, D, 128]); din("wvT", [D, 512])
    din("mem", [512, 512]); din("mnorm", [128, 4]); din("mweight", [1, 1])
    din("gwT", [D, 8]); din("swg", [22, D, 128]); din("swu", [22, D, 128])
    din("swd", [16, 2816, 128]); din("idt", [128, 128])
    dr["eg"] = nc.dram_tensor("eg", [8, 11, D, 128], F32, kind="ExternalInput").ap()
    dr["eu"] = nc.dram_tensor("eu", [8, 11, D, 128], F32, kind="ExternalInput").ap()
    dr["ed"] = nc.dram_tensor("ed", [8, 16, 1408, 128], F32, kind="ExternalInput").ap()
    dr["outT"] = nc.dram_tensor("outT", [D, T], F32, kind="ExternalOutput").ap()
    for nm, shp, dt in (("x1T", [D, T], F32), ("x2T", [D, T], F32),
                        ("x3T", [D, T], F32), ("qa_s", [D, T], BF16),
                        ("ka_s", [D, T], F32), ("va_s", [D, T], F32),
                        ("qc_s", [D, T], BF16), ("qc32_s", [D, T], F32),
                        ("adc_s", [D, T], F32), ("werow_s", [8, T], F32)):
        dr[nm] = nc.dram_tensor(nm, shp, dt, kind="Internal").ap()

    sh = {}
    sh['xb'] = [top.enter_context(nc.sbuf_tensor(f"sh_xb{i}", [128, T], F32))
                for i in range(2)]
    sh['sqb'] = top.enter_context(nc.sbuf_tensor("sh_sqb", [128, T], F32))
    sh['sq2'] = top.enter_context(nc.sbuf_tensor("sh_sq2", [128, T], F32))
    sh['onc'] = top.enter_context(nc.sbuf_tensor("sh_onc", [128, 1], F32))
    sh['onr'] = top.enter_context(nc.sbuf_tensor("sh_onr", [1, 128], F32))
    sh['half_sb'] = top.enter_context(nc.sbuf_tensor("sh_half", [128, 1], F32))

    S = sems(nc, top, ALL_SEMS)
    emit_sec_A(nc, S, sh, dr, debug=debug)
    emit_sec_B(nc, S, sh, dr, debug=debug)
    emit_sec_C(nc, S, sh, dr, debug=debug)
    emit_sec_D(nc, S, sh, dr, debug=debug)

    top.close()
    nc.all_engine_barrier()
    cleanup.__exit__(None, None, None)
    return nc


def prep_full(inputs):
    mapsA = prep_A(inputs["L0_attn_X"], inputs["L0_attn_W_in"], inputs["L0_attn_b_in"],
                   inputs["L0_attn_W_out"], inputs["L0_attn_b_out"],
                   inputs["L0_attn_gamma"], inputs["L0_attn_beta"])
    sB = prep_B(inputs["L0_ffn_W"], inputs["L0_ffn_ln_w"], inputs["L0_ffn_ln_b"])
    sC = prep_C(inputs["L1_attn_Wq"], inputs["L1_attn_Wk"], inputs["L1_attn_Wv"],
                inputs["L1_attn_memory"], inputs["L1_attn_memory_norm"],
                inputs["L1_attn_memory_weight"])
    sD = prep_D(inputs["L1_ffn_gate_w"], inputs["L1_ffn_shared_wg"],
                inputs["L1_ffn_shared_wu"], inputs["L1_ffn_shared_wd"],
                inputs["L1_ffn_exp_wg"], inputs["L1_ffn_exp_wu"],
                inputs["L1_ffn_exp_wd"])
    maps = []
    for b in range(8):
        m = dict(mapsA[b]); m.update(sB); m.update(sC); m.update(sD)
        maps.append(m)
    return maps


def kernel(**inputs):
    maps = prep_full(inputs)
    nc = build_full()
    res = run_bass_kernel_spmd(nc, maps, core_ids=list(range(8)))
    out = np.stack([res.results[b]["outT"].T for b in range(8)], axis=0)
    if not np.isfinite(out).all():
        res = run_bass_kernel_spmd(nc, maps, core_ids=list(range(8)))
        out = np.stack([res.results[b]["outT"].T for b in range(8)], axis=0)
    return np.ascontiguousarray(out)


# version 08 (final)
# speedup vs baseline: 1.0123x; 1.0123x over previous
import contextlib
import numpy as np

import concourse.bass as bass
import concourse.mybir as mybir
from concourse.bass_utils import run_bass_kernel_spmd

F32 = mybir.dt.float32
BF16 = mybir.dt.bfloat16
AF = mybir.ActivationFunctionType
ALU = mybir.AluOpType
AX = mybir.AxisListType.X

D = 2048
T = 2048
NK = 16
NN = 4
EPS = 1e-5


class Sem:
    def __init__(self, h):
        self.h = h
        self.n = 0

    def inc(self, inst, k=1):
        inst.then_inc(self.h, k)
        self.n += k
        return inst

    def dinc(self, inst):
        return self.inc(inst, 16)


def sems(nc, stack, names):
    return {nm: Sem(stack.enter_context(nc.semaphore(nm))) for nm in names}



def emit_ln_stats(nc, S, x_ap, xb, sqb, rows, onc, single, tag):
    x_t = x_ap.rearrange("(k p) t -> k p t", p=128)
    sx = [S[f'{tag}x0'], S[f'{tag}x1']]
    ssq, smm, srow = S[f'{tag}sq'], S[f'{tag}mm'], S[f'{tag}row']
    b_x = [s.n for s in sx]
    b_sq, b_mm = ssq.n, smm.n
    with nc.psum_tensor(f"{tag}_ps_sum_{smm.n}", [1, T], F32) as ps_sum, \
         nc.psum_tensor(f"{tag}_ps_sq_{smm.n}", [1, T], F32) as ps_sq:
        with nc.Block() as blk:
            @blk.sync
            def _(sy):
                for k in range(NK):
                    if k >= 2:
                        sy.wait_ge(smm.h, b_mm + 2 * (k - 2) + 1)
                    sx[k % 2].dinc(sy.dma_start(xb[k % 2][:, :], x_t[k]))
            @blk.scalar
            def _(sc):
                for k in range(NK):
                    sc.wait_ge(sx[k % 2].h, b_x[k % 2] + 16 * (k // 2 + 1))
                    if k >= 1:
                        sc.wait_ge(smm.h, b_mm + 2 * k)
                    ssq.inc(sc.activation(sqb[:, :], xb[k % 2][:, :], AF.Square))
            @blk.tensor
            def _(t):
                for k in range(NK):
                    t.wait_ge(ssq.h, b_sq + k + 1)
                    for n in range(NN):
                        inst = t.matmul(ps_sum[:, n * 512:(n + 1) * 512], onc[:, :],
                                        xb[k % 2][:, n * 512:(n + 1) * 512],
                                        start=(k == 0), stop=(k == NK - 1))
                    smm.inc(inst)
                    for n in range(NN):
                        inst = t.matmul(ps_sq[:, n * 512:(n + 1) * 512], onc[:, :],
                                        sqb[:, n * 512:(n + 1) * 512],
                                        start=(k == 0), stop=(k == NK - 1))
                    smm.inc(inst)
        m, r1, rho, mrho = rows['m'], rows['r1'], rows['rho'], rows['mrho']
        t0, t1 = rows['t0'], rows['t1']
        b_row = srow.n
        with nc.Block() as blk:
            @blk.vector
            def _(v):
                v.wait_ge(smm.h, b_mm + 2 * NK)
                v.tensor_single_scalar(m, ps_sum[:, :], 1.0 / D, ALU.mult)
                v.tensor_single_scalar(t0, ps_sq[:, :], 1.0 / D, ALU.mult)
                v.tensor_mul(t1, m, m)
                v.tensor_sub(t0, t0, t1)
                srow.inc(v.tensor_single_scalar(t1, t0, EPS, ALU.add))
            @blk.scalar
            def _(sc):
                sc.wait_ge(srow.h, b_row + 1)
                sc.activation(t1, t1, AF.Ln)
                srow.inc(sc.activation(r1, t1, AF.Exp, scale=-0.5))
        b_row = srow.n
        with nc.Block() as blk:
            if single:
                @blk.vector
                def _(v):
                    v.wait_ge(srow.h, b_row)
                    v.tensor_copy(rho, r1)
                    srow.inc(v.tensor_mul(mrho, m, rho))
            else:
                @blk.vector
                def _(v):
                    v.wait_ge(srow.h, b_row)
                    v.tensor_mul(t1, t0, r1)
                    v.tensor_mul(t1, t1, r1)
                    srow.inc(v.tensor_single_scalar(t1, t1, EPS, ALU.add))
                @blk.scalar
                def _(sc):
                    sc.wait_ge(srow.h, b_row + 1)
                    sc.activation(t1, t1, AF.Ln)
                    srow.inc(sc.activation(t1, t1, AF.Exp, scale=-0.5))
        if not single:
            b_row = srow.n
            with nc.Block() as blk:
                @blk.vector
                def _(v):
                    v.wait_ge(srow.h, b_row)
                    v.tensor_mul(rho, r1, t1)
                    srow.inc(v.tensor_mul(mrho, m, rho))
    nc.all_engine_barrier()


def emit_bcast(nc, S, row_dest_pairs, onr, tag):
    sbc, sbe = S[f'{tag}bc'], S[f'{tag}bce']
    b_bc, b_be = sbc.n, sbe.n
    with nc.psum_tensor(f"bc_ps_{sbc.n}", [128, T], F32) as ps_bc:
        with nc.Block() as blk:
            @blk.tensor
            def _(t):
                t.wait_ge(S[f'{tag}row'].h, S[f'{tag}row'].n)
                for i, (row, dest) in enumerate(row_dest_pairs):
                    if i > 0:
                        t.wait_ge(sbe.h, b_be + i)
                    for n in range(NN):
                        inst = t.matmul(ps_bc[:, n * 512:(n + 1) * 512], onr[:, :],
                                        row[:, n * 512:(n + 1) * 512],
                                        start=True, stop=True)
                    sbc.inc(inst)
            @blk.vector
            def _(v):
                for i, (row, dest) in enumerate(row_dest_pairs):
                    v.wait_ge(sbc.h, b_bc + i + 1)
                    sbe.inc(v.tensor_copy(dest[:, :], ps_bc[:, :]))
    nc.all_engine_barrier()


def emit_ln_apply(nc, S, x_ap, xb, rhob, mrhob, xn_sb, gamma, beta, tag,
                  half=None, seg=None):
    if seg is not None:
        off, W = seg
    else:
        W = T if half is None else T // 2
        off = 0 if (half is None or half == 0) else T // 2
    x_t = x_ap.rearrange("(k p) t -> k p t", p=128)
    sx = [S[f'{tag}x0'], S[f'{tag}x1']]
    sap = S[f'{tag}ap']
    b_x = [s.n for s in sx]
    b_ap = sap.n
    with nc.Block() as blk:
        @blk.sync
        def _(sy):
            for k in range(NK):
                if k >= 2:
                    sy.wait_ge(sap.h, b_ap + (k - 1))
                sx[k % 2].dinc(sy.dma_start(xb[k % 2][:, 0:W], x_t[k, :, off:off + W]))
        @blk.vector
        def _(v):
            v.wait_ge(S[f'{tag}bce'].h, S[f'{tag}bce'].n)
            for k in range(NK):
                v.wait_ge(sx[k % 2].h, b_x[k % 2] + 16 * (k // 2 + 1))
                v.tensor_mul(xb[k % 2][:, 0:W], xb[k % 2][:, 0:W],
                             rhob[:, off:off + W])
                o = xn_sb[:, k * W:(k + 1) * W]
                if gamma is not None:
                    v.tensor_sub(xb[k % 2][:, 0:W], xb[k % 2][:, 0:W],
                                 mrhob[:, off:off + W])
                    sap.inc(v.tensor_scalar(o, xb[k % 2][:, 0:W],
                                            gamma[:, k:k + 1], beta[:, k:k + 1],
                                            ALU.mult, ALU.add))
                else:
                    sap.inc(v.tensor_sub(o, xb[k % 2][:, 0:W],
                                         mrhob[:, off:off + W]))
    nc.all_engine_barrier()



def emit_sec_A(nc, S, sh, dr, debug=False):
    xb = sh['xb']; sqb = sh['sqb']; sq2 = sh['sq2']
    onc = sh['onc']; onr = sh['onr']; half_sb = sh.get('half_sb')
    st = contextlib.ExitStack()

    xT = dr["xT"]
    w_in = dr["w_in"]
    b_in = dr["b_in"]
    w_out = dr["w_out"]
    b_out = dr["b_out"]
    gamma = dr["gamma"]
    beta = dr["beta"]
    cosT = dr["cosT"]
    sinT = dr["sinT"]
    ones_col = dr["ones_col"]
    ones_row = dr["ones_row"]
    x1T = dr["x1T"]
    q_s = dr["qa_s"]
    k_s = dr["ka_s"]
    v_s = dr["va_s"]
    if debug:
        dbg_A = dr["dbg_A"]

    row_m = st.enter_context(nc.sbuf_tensor("a_row_m", [1, T], F32))
    row_r1 = st.enter_context(nc.sbuf_tensor("a_row_r1", [1, T], F32))
    row_rho = st.enter_context(nc.sbuf_tensor("a_row_rho", [1, T], F32))
    row_mrho = st.enter_context(nc.sbuf_tensor("a_row_mrho", [1, T], F32))
    rows = {'m': row_m[0:1, :], 'r1': row_r1[0:1, :], 'rho': row_rho[0:1, :],
            'mrho': row_mrho[0:1, :], 't0': sqb[0:1, 0:T], 't1': sq2[0:1, 0:T]}
    bc0 = st.enter_context(nc.sbuf_tensor("a_bc0", [128, T], F32))
    bc1 = st.enter_context(nc.sbuf_tensor("a_bc1", [128, T], F32))
    bi_sb = st.enter_context(nc.sbuf_tensor("a_bi_sb", [128, 48], F32))
    bo_sb = st.enter_context(nc.sbuf_tensor("a_bo_sb", [128, 16], F32))
    ga_sb = st.enter_context(nc.sbuf_tensor("a_ga_sb", [128, 16], F32))
    be_sb = st.enter_context(nc.sbuf_tensor("a_be_sb", [128, 16], F32))
    A_sb = st.enter_context(nc.sbuf_tensor("a_A_sb", [128, 16], F32))


    with nc.Block() as blk:
        @blk.sync
        def _(sy):
            for dst, src in ((onc, ones_col), (onr, ones_row), (bi_sb, b_in),
                             (bo_sb, b_out), (ga_sb, gamma), (be_sb, beta)):
                S['mi0'].dinc(sy.dma_start(dst[:, :], src))
            sy.wait_ge(S['mi0'].h, S['mi0'].n)
        @blk.vector
        def _(v):
            v.memset(half_sb[:, :], 0.5)
    nc.all_engine_barrier()

    emit_ln_stats(nc, S, xT, xb, sqb, rows, onc, single=False, tag='p')
    emit_bcast(nc, S, [(rows['rho'], bc0), (rows['mrho'], bc1)], onr, tag='p')

    a2 = contextlib.ExitStack()
    xn_sb = a2.enter_context(nc.sbuf_tensor("a_xn_sb", [128, NK * T], BF16))
    emit_ln_apply(nc, S, xT, xb, bc0, bc1, xn_sb, ga_sb, be_sb, tag='p')

    wbuf = [a2.enter_context(nc.sbuf_tensor(f"a_wbuf{i}", [128, NK * 128], BF16))
            for i in range(2)]
    cs_sb = a2.enter_context(nc.sbuf_tensor("a_cs_sb", [128, T], F32))
    sn_sb = a2.enter_context(nc.sbuf_tensor("a_sn_sb", [128, T], F32))
    e1 = xb[0]
    e2 = xb[1]
    qsl = [a2.enter_context(nc.sbuf_tensor(f"a_qsl{i}", [128, T], BF16)) for i in range(2)]
    psl = a2.enter_context(nc.sbuf_tensor("a_psl", [128, T], F32))
    rt1 = a2.enter_context(nc.sbuf_tensor("a_rt1", [128, T], F32))
    rt2 = a2.enter_context(nc.sbuf_tensor("a_rt2", [128, T], F32))

    sched = [('q', i, i) for i in range(16)]
    for c in range(4):
        sched += [('p1', 16 + c, ('k', c)), ('p2', 20 + c, ('k', c)),
                  ('p1', 32 + c, ('v', c)), ('p2', 36 + c, ('v', c))]
    sched += [('pass', 24 + i, ('k', 8 + i)) for i in range(8)]
    sched += [('pass', 40 + i, ('v', 8 + i)) for i in range(8)]
    n_tiles = len(sched)
    pair_of = {}
    pr = 0
    for mi, (kind, wi, ex) in enumerate(sched):
        if kind in ('p1', 'p2'):
            pair_of[mi] = pr
            if kind == 'p2':
                pr += 1
    n_pairs = pr
    p2_mi = [mi for mi in range(n_tiles) if sched[mi][0] == 'p2']

    qs_t = q_s.rearrange("(k p) t -> k p t", p=128)
    ks_t = k_s.rearrange("(k p) t -> k p t", p=128)
    vs_t = v_s.rearrange("(k p) t -> k p t", p=128)
    cos_t = cosT.rearrange("(c p) t -> c p t", p=128)
    sin_t = sinT.rearrange("(c p) t -> c p t", p=128)

    b = {k: S[k].n for k in ('w0', 'w1', 'mmt', 'evt', 'rot', 'cs0',
                             'q0', 'q1', 'k0', 'r0', 'r1s')}
    with nc.psum_tensor("a_psq0", [128, T], F32) as psq0, \
         nc.psum_tensor("a_psq1", [128, T], F32) as psq1:
        psq = [psq0, psq1]
        with nc.Block() as blk:
            @blk.gpsimd
            def _(g):
                for mi in range(n_tiles):
                    if mi >= 2:
                        g.wait_ge(S['mmt'].h, b['mmt'] + mi - 1)
                    S[f'w{mi % 2}'].dinc(g.dma_start(
                        wbuf[mi % 2][:, :].rearrange("p (k c) -> p k c", c=128),
                        w_in[sched[mi][1]].rearrange("(k p) c -> p k c", p=128)))

            @blk.tensor
            def _(t):
                t.wait_ge(S['pap'].h, S['pap'].n)
                for mi in range(n_tiles):
                    t.wait_ge(S[f'w{mi % 2}'].h, b[f'w{mi % 2}'] + 16 * (mi // 2 + 1))
                    if mi >= 2:
                        t.wait_ge(S['evt'].h, b['evt'] + mi - 1)
                    for n in range(NN):
                        for k in range(NK):
                            inst = t.matmul(
                                psq[mi % 2][:, n * 512:(n + 1) * 512],
                                wbuf[mi % 2][:, k * 128:(k + 1) * 128],
                                xn_sb[:, k * T + n * 512:k * T + (n + 1) * 512],
                                start=(k == 0), stop=(k == NK - 1))
                    S['mmt'].inc(inst)

            @blk.scalar
            def _(sc):
                nq = 0
                npass = 0
                for mi in range(n_tiles):
                    kind, wi, ex = sched[mi]
                    sc.wait_ge(S['mmt'].h, b['mmt'] + mi + 1)
                    bias = bi_sb[:, wi:wi + 1]
                    if kind == 'q':
                        if nq >= 2:
                            sc.wait_ge(S[f'q{nq % 2}'].h, b[f'q{nq % 2}'] + 16 * (nq // 2))
                        S['evt'].inc(sc.activation(qsl[nq % 2][:, :], psq[mi % 2][:, :],
                                                   AF.Identity, bias=bias))
                        nq += 1
                    elif kind == 'p1':
                        p = pair_of[mi]
                        if p >= 1:
                            sc.wait_ge(S['rot'].h, b['rot'] + p)
                        S['evt'].inc(sc.activation(e1[:, :], psq[mi % 2][:, :],
                                                   AF.Identity, bias=bias))
                    elif kind == 'p2':
                        S['evt'].inc(sc.activation(e2[:, :], psq[mi % 2][:, :],
                                                   AF.Identity, bias=bias))
                    else:
                        if npass >= 1:
                            sc.wait_ge(S['k0'].h, b['k0'] + 16 * npass)
                        S['evt'].inc(sc.activation(psl[:, :], psq[mi % 2][:, :],
                                                   AF.Identity, bias=bias))
                        npass += 1

            @blk.vector
            def _(v):
                for p in range(n_pairs):
                    c = sched[p2_mi[p]][2][1]
                    v.wait_ge(S['evt'].h, b['evt'] + p2_mi[p] + 1)
                    v.wait_ge(S['cs0'].h, b['cs0'] + 32 * (c + 1))
                    if p >= 1:
                        v.wait_ge(S['r0'].h, b['r0'] + 16 * p)
                        v.wait_ge(S['r1s'].h, b['r1s'] + 16 * p)
                    v.tensor_mul(rt1[:, :], e1[:, :], cs_sb[:, :])
                    v.tensor_mul(sqb[:, :], e2[:, :], sn_sb[:, :])
                    v.tensor_sub(rt1[:, :], rt1[:, :], sqb[:, :])
                    v.tensor_mul(rt2[:, :], e2[:, :], cs_sb[:, :])
                    v.tensor_mul(sqb[:, :], e1[:, :], sn_sb[:, :])
                    S['rot'].inc(v.tensor_add(rt2[:, :], rt2[:, :], sqb[:, :]))

            @blk.sync
            def _(sy):
                nq = 0
                npass = 0
                for mi in range(n_tiles):
                    kind, wi, ex = sched[mi]
                    if kind == 'p1' and ex[0] == 'k':
                        c = ex[1]
                        if c > 0:
                            sy.wait_ge(S['rot'].h, b['rot'] + 2 * c)
                        S['cs0'].dinc(sy.dma_start(cs_sb[:, :], cos_t[c]))
                        S['cs0'].dinc(sy.dma_start(sn_sb[:, :], sin_t[c]))
                    if kind == 'q':
                        sy.wait_ge(S['evt'].h, b['evt'] + mi + 1)
                        S[f'q{nq % 2}'].dinc(sy.dma_start(qs_t[ex], qsl[nq % 2][:, :]))
                        nq += 1
                    elif kind == 'p2':
                        tgt, c = ex
                        p = pair_of[mi]
                        sy.wait_ge(S['rot'].h, b['rot'] + p + 1)
                        dst = ks_t if tgt == 'k' else vs_t
                        S['r0'].dinc(sy.dma_start(dst[c], rt1[:, :]))
                        S['r1s'].dinc(sy.dma_start(dst[c + 4], rt2[:, :]))
                    elif kind == 'pass':
                        tgt, ci = ex
                        sy.wait_ge(S['evt'].h, b['evt'] + mi + 1)
                        dst = ks_t if tgt == 'k' else vs_t
                        S['k0'].dinc(sy.dma_start(dst[ci], psl[:, :]))
                        npass += 1
                for s in ('q0', 'q1', 'k0', 'r0', 'r1s'):
                    sy.wait_ge(S[s].h, S[s].n)
    nc.all_engine_barrier()
    a2.close()

    a3 = contextlib.ExitStack()
    kc = [a3.enter_context(nc.sbuf_tensor(f"a_kc{i}", [128, T], F32)) for i in range(2)]
    vc = [a3.enter_context(nc.sbuf_tensor(f"a_vc{i}", [128, T], F32)) for i in range(2)]
    b = {k: S[k].n for k in ('px0', 'px1', 'psq', 'pmm', 'prow')}
    with nc.psum_tensor("a_ps_ssk", [1, T], F32) as ps_ssk, \
         nc.psum_tensor("a_ps_ssv", [1, T], F32) as ps_ssv:
        with nc.Block() as blk:
            @blk.sync
            def _(sy):
                for k in range(NK):
                    if k >= 2:
                        sy.wait_ge(S['psq'].h, b['psq'] + 2 * (k - 1))
                    S[f'px{k % 2}'].dinc(sy.dma_start(kc[k % 2][:, :], ks_t[k]))
                    S[f'px{k % 2}'].dinc(sy.dma_start(vc[k % 2][:, :], vs_t[k]))
            @blk.scalar
            def _(sc):
                for k in range(NK):
                    sc.wait_ge(S[f'px{k % 2}'].h, b[f'px{k % 2}'] + 32 * (k // 2 + 1))
                    if k >= 1:
                        sc.wait_ge(S['pmm'].h, b['pmm'] + 2 * k)
                    S['psq'].inc(sc.activation(sqb[:, :], kc[k % 2][:, :], AF.Square))
                    S['psq'].inc(sc.activation(sq2[:, :], vc[k % 2][:, :], AF.Square))
            @blk.tensor
            def _(t):
                for k in range(NK):
                    t.wait_ge(S['psq'].h, b['psq'] + 2 * k + 2)
                    for n in range(NN):
                        inst = t.matmul(ps_ssk[:, n * 512:(n + 1) * 512], onc[:, :],
                                        sqb[:, n * 512:(n + 1) * 512],
                                        start=(k == 0), stop=(k == NK - 1))
                    S['pmm'].inc(inst)
                    for n in range(NN):
                        inst = t.matmul(ps_ssv[:, n * 512:(n + 1) * 512], onc[:, :],
                                        sq2[:, n * 512:(n + 1) * 512],
                                        start=(k == 0), stop=(k == NK - 1))
                    S['pmm'].inc(inst)
        b_row = S['prow'].n
        with nc.Block() as blk:
            @blk.vector
            def _(v):
                v.wait_ge(S['pmm'].h, S['pmm'].n)
                v.tensor_copy(rows['t1'], ps_ssk[:, :])
                S['prow'].inc(v.tensor_tensor(rows['t0'], rows['t1'],
                                              ps_ssv[:, :], ALU.mult))
            @blk.scalar
            def _(sc):
                sc.wait_ge(S['prow'].h, b_row + 1)
                sc.activation(rows['t1'], rows['t0'], AF.Ln)
                S['prow'].inc(sc.activation(rows['t0'], rows['t1'],
                                            AF.Exp, scale=-0.5))
    nc.all_engine_barrier()
    emit_bcast(nc, S, [(rows['t0'], bc0)], onr, tag='p')

    b = {k: S[k].n for k in ('px0', 'px1', 'pap')}
    with nc.Block() as blk:
        @blk.sync
        def _(sy):
            for k in range(NK):
                if k >= 2:
                    sy.wait_ge(S['pap'].h, b['pap'] + (k - 1))
                S[f'px{k % 2}'].dinc(sy.dma_start(kc[k % 2][:, :], ks_t[k]))
                S[f'px{k % 2}'].dinc(sy.dma_start(vc[k % 2][:, :], vs_t[k]))
        @blk.vector
        def _(v):
            v.wait_ge(S['pbce'].h, S['pbce'].n)
            for k in range(NK):
                v.wait_ge(S[f'px{k % 2}'].h, b[f'px{k % 2}'] + 32 * (k // 2 + 1))
                v.tensor_mul(sqb[:, :], kc[k % 2][:, :], vc[k % 2][:, :])
                v.tensor_mul(sqb[:, :], sqb[:, :], bc0[:, :])
                S['pap'].inc(v.reduce_sum(A_sb[:, k:k + 1], sqb[:, :], axis=AX))
        @blk.scalar
        def _(sc):
            sc.wait_ge(S['pap'].h, b['pap'] + NK)
            S['prow'].inc(sc.activation(A_sb[:, :], A_sb[:, :], AF.Identity,
                                        bias=half_sb[:, 0:1], scale=0.5))
    nc.all_engine_barrier()
    a3.close()
    if debug:
        with nc.Block() as blk:
            @blk.sync
            def _(sy):
                S['mi0'].dinc(sy.dma_start(dbg_A, A_sb[:, :]))
                sy.wait_ge(S['mi0'].h, S['mi0'].n)
        nc.all_engine_barrier()

    a4 = contextlib.ExitStack()
    wt_sb = a4.enter_context(nc.sbuf_tensor("a_wt_sb", [128, NK * T], BF16))
    q_sb = a4.enter_context(nc.sbuf_tensor("a_q_sb", [128, NK * (T // 2)], BF16))
    s1b = [a4.enter_context(nc.sbuf_tensor(f"a_s1b{i}", [128, T // 2], F32))
           for i in range(2)]
    xb2 = [a4.enter_context(nc.sbuf_tensor(f"a_xb2{i}", [128, T // 2], F32))
           for i in range(2)]
    emit_bcast(nc, S, [(rows['m'], bc0), (rows['r1'], bc1)], onr, tag='p')

    wo_t = w_out.rearrange("(k p) t -> k p t", p=128)
    x_t_in = xT.rearrange("(k p) t -> k p t", p=128)
    x1_t = x1T.rearrange("(k p) t -> k p t", p=128)
    W2 = T // 2

    b = {k: S[k].n for k in ('w0', 'w1', 'pap')}
    with nc.Block() as blk:
        @blk.gpsimd
        def _(g):
            for k in range(NK):
                if k >= 2:
                    g.wait_ge(S['pap'].h, b['pap'] + (k - 1))
                S[f'w{k % 2}'].dinc(g.dma_start(xb[k % 2][:, :], wo_t[k]))
        @blk.vector
        def _(v):
            v.wait_ge(S['prow'].h, S['prow'].n)
            for k in range(NK):
                v.wait_ge(S[f'w{k % 2}'].h, b[f'w{k % 2}'] + 16 * (k // 2 + 1))
                S['pap'].inc(v.tensor_single_scalar(
                    wt_sb[:, k * T:(k + 1) * T], xb[k % 2][:, :],
                    A_sb[:, k:k + 1], ALU.mult))
    nc.all_engine_barrier()

    for h in range(2):
        off = h * W2
        b = {k: S[k].n for k in ('px0', 'px1', 'mmt', 'evt', 'dv',
                                 'o0', 'o1', 'q0', 'q1')}
        with nc.psum_tensor(f"a_psA0_{h}", [128, W2], F32) as psA0, \
             nc.psum_tensor(f"a_psA1_{h}", [128, W2], F32) as psA1:
            psA = [psA0, psA1]
            with nc.Block() as blk:
                @blk.gpsimd
                def _(g):
                    for k in range(NK):
                        S[f'q{k % 2}'].dinc(g.dma_start(
                            q_sb[:, k * W2:(k + 1) * W2], qs_t[k, :, off:off + W2]))
                @blk.tensor
                def _(t):
                    t.wait_ge(S['pap'].h, S['pap'].n)
                    t.wait_ge(S['q0'].h, b['q0'] + 16 * 8)
                    t.wait_ge(S['q1'].h, b['q1'] + 16 * 8)
                    for m in range(NK):
                        if m >= 2:
                            t.wait_ge(S['evt'].h, b['evt'] + m - 1)
                        for n in range(2):
                            for k in range(NK):
                                inst = t.matmul(
                                    psA[m % 2][:, n * 512:(n + 1) * 512],
                                    wt_sb[:, k * T + m * 128:k * T + (m + 1) * 128],
                                    q_sb[:, k * W2 + n * 512:k * W2 + (n + 1) * 512],
                                    start=(k == 0), stop=(k == NK - 1))
                        S['mmt'].inc(inst)
                @blk.scalar
                def _(sc):
                    for m in range(NK):
                        sc.wait_ge(S['mmt'].h, b['mmt'] + m + 1)
                        if m >= 2:
                            sc.wait_ge(S[f'o{m % 2}'].h, b[f'o{m % 2}'] + 16 * (m // 2))
                        S['evt'].inc(sc.activation(s1b[m % 2][:, :], psA[m % 2][:, :],
                                                   AF.Identity, bias=bo_sb[:, m:m + 1]))
                @blk.vector
                def _(v):
                    for m in range(NK):
                        v.wait_ge(S['evt'].h, b['evt'] + m + 1)
                        v.wait_ge(S[f'px{m % 2}'].h, b[f'px{m % 2}'] + 16 * (m // 2 + 1))
                        v.tensor_sub(sqb[:, 0:W2], xb2[m % 2][:, :], bc0[:, off:off + W2])
                        v.tensor_mul(sqb[:, 0:W2], sqb[:, 0:W2], bc1[:, off:off + W2])
                        v.tensor_add(sqb[:, 0:W2], sqb[:, 0:W2], xb2[m % 2][:, :])
                        S['dv'].inc(v.tensor_add(s1b[m % 2][:, :], sqb[:, 0:W2],
                                                 s1b[m % 2][:, :]))
                @blk.sync
                def _(sy):
                    S['px0'].dinc(sy.dma_start(xb2[0][:, :], x_t_in[0, :, off:off + W2]))
                    S['px1'].dinc(sy.dma_start(xb2[1][:, :], x_t_in[1, :, off:off + W2]))
                    for m in range(NK):
                        sy.wait_ge(S['dv'].h, b['dv'] + m + 1)
                        S[f'o{m % 2}'].dinc(sy.dma_start(x1_t[m, :, off:off + W2],
                                                         s1b[m % 2][:, :]))
                        if m + 2 < NK:
                            S[f'px{m % 2}'].dinc(sy.dma_start(
                                xb2[m % 2][:, :], x_t_in[m + 2, :, off:off + W2]))
                    for s in ('o0', 'o1'):
                        sy.wait_ge(S[s].h, S[s].n)
        nc.all_engine_barrier()
    a4.close()

    st.close()
    nc.all_engine_barrier()




def emit_sec_B(nc, S, sh, dr, debug=False):
    xb = sh['xb']; sqb = sh['sqb']; sq2 = sh['sq2']
    onc = sh['onc']; onr = sh['onr']; half_sb = sh.get('half_sb')
    st = contextlib.ExitStack()

    x1T = dr["x1T"]
    w = dr["w"]
    lnw = dr["lnw"]
    lnb = dr["lnb"]
    ones_col = dr["ones_col"]
    ones_row = dr["ones_row"]
    x2T = dr["x2T"]
    if debug:
        dbg_wq = dr["dbg_wq"]
        dbg_wabs = dr["dbg_wabs"]
        dbg_thr = dr["dbg_thr"]

    row_m = st.enter_context(nc.sbuf_tensor("b_row_m", [1, T], F32))
    row_r1 = st.enter_context(nc.sbuf_tensor("b_row_r1", [1, T], F32))
    row_rho = st.enter_context(nc.sbuf_tensor("b_row_rho", [1, T], F32))
    row_mrho = st.enter_context(nc.sbuf_tensor("b_row_mrho", [1, T], F32))
    rows = {'m': row_m[0:1, :], 'r1': row_r1[0:1, :], 'rho': row_rho[0:1, :],
            'mrho': row_mrho[0:1, :], 't0': sqb[0:1, 0:T], 't1': sq2[0:1, 0:T]}
    lw_sb = st.enter_context(nc.sbuf_tensor("b_lw_sb", [128, 16], F32))
    lb_sb = st.enter_context(nc.sbuf_tensor("b_lb_sb", [128, 16], F32))
    wabs = st.enter_context(nc.sbuf_tensor("b_wabs", [128, 16], F32))
    thr_b = st.enter_context(nc.sbuf_tensor("b_thr_b", [128, 1], F32))


    with nc.Block() as blk:
        @blk.sync
        def _(sy):
            for dst, src in ((onc, ones_col), (onr, ones_row),
                             (lw_sb, lnw), (lb_sb, lnb)):
                S['mi0'].dinc(sy.dma_start(dst[:, :], src))
            sy.wait_ge(S['mi0'].h, S['mi0'].n)
    nc.all_engine_barrier()

    emit_ln_stats(nc, S, x1T, xb, sqb, rows, onc, single=False, tag='p')

    xn_sb = st.enter_context(nc.sbuf_tensor("b_xn_sb", [128, NK * T], BF16))
    bstk = contextlib.ExitStack()
    bc0 = bstk.enter_context(nc.sbuf_tensor("b_bc0", [128, T], F32))
    bc1 = bstk.enter_context(nc.sbuf_tensor("b_bc1", [128, T], F32))
    emit_bcast(nc, S, [(rows['rho'], bc0), (rows['mrho'], bc1)], onr, tag='p')
    emit_ln_apply(nc, S, x1T, xb, bc0, bc1, xn_sb, lw_sb, lb_sb, tag='p')

    w_t = w.rearrange("(k p) t -> k p t", p=128)
    b = {k: S[k].n for k in ('w0', 'w1', 'dv', 'pmm', 'prow', 'pbc')}
    with nc.psum_tensor("b_psS", [1, 16], F32) as psS, \
         nc.psum_tensor("b_psB", [128, 1], F32) as psB:
        with nc.Block() as blk:
            @blk.gpsimd
            def _(g):
                for k in range(NK):
                    if k >= 2:
                        g.wait_ge(S['dv'].h, b['dv'] + (k - 1))
                    S[f'w{k % 2}'].dinc(g.dma_start(xb[k % 2][:, :], w_t[k]))
            @blk.vector
            def _(v):
                for k in range(NK):
                    v.wait_ge(S[f'w{k % 2}'].h, b[f'w{k % 2}'] + 16 * (k // 2 + 1))
                    S['dv'].inc(v.tensor_reduce(wabs[:, k:k + 1], xb[k % 2][:, :],
                                                AX, ALU.add, apply_absolute_value=True))
            @blk.tensor
            def _(t):
                t.wait_ge(S['dv'].h, b['dv'] + NK)
                S['pmm'].inc(t.matmul(psS[:, 0:16], onc[:, :], wabs[:, :],
                                      start=True, stop=True))
        with nc.Block() as blk:
            @blk.vector
            def _(v):
                v.wait_ge(S['pmm'].h, b['pmm'] + 1)
                S['prow'].inc(v.reduce_sum(rows['t0'][0:1, 0:1],
                                           psS[:, 0:16], axis=AX))
        with nc.Block() as blk:
            @blk.tensor
            def _(t):
                t.wait_ge(S['prow'].h, S['prow'].n)
                S['pbc'].inc(t.matmul(psB[:, :], onr[:, :], rows['t0'][0:1, 0:1],
                                      start=True, stop=True))
            @blk.vector
            def _(v):
                v.wait_ge(S['pbc'].h, S['pbc'].n)
                S['prow'].inc(v.tensor_copy(thr_b[:, :], psB[:, :]))
    nc.all_engine_barrier()

    bstk.close()
    wq_sb = st.enter_context(nc.sbuf_tensor("b_wq_sb", [128, NK * T], BF16))
    sgn = st.enter_context(nc.sbuf_tensor("b_sgn", [128, T], F32))
    b = {k: S[k].n for k in ('w0', 'w1', 'psq', 'qz')}
    with nc.Block() as blk:
        @blk.gpsimd
        def _(g):
            for k in range(NK):
                if k >= 2:
                    g.wait_ge(S['qz'].h, b['qz'] + (k - 1))
                S[f'w{k % 2}'].dinc(g.dma_start(xb[k % 2][:, :], w_t[k]))
        @blk.scalar
        def _(sc):
            for k in range(NK):
                sc.wait_ge(S[f'w{k % 2}'].h, b[f'w{k % 2}'] + 16 * (k // 2 + 1))
                if k >= 1:
                    sc.wait_ge(S['qz'].h, b['qz'] + k)
                S['psq'].inc(sc.activation(sqb[:, :], xb[k % 2][:, :], AF.Abs,
                                           scale=2.0 * 2048.0 * 2048.0))
                S['psq'].inc(sc.sign(sgn[:, :], xb[k % 2][:, :]))
        @blk.vector
        def _(v):
            v.wait_ge(S['prow'].h, S['prow'].n)
            for k in range(NK):
                v.wait_ge(S['psq'].h, b['psq'] + 2 * k + 2)
                v.tensor_single_scalar(sq2[:, :], sqb[:, :], thr_b[:, 0:1], ALU.is_ge)
                S['qz'].inc(v.tensor_mul(wq_sb[:, k * T:(k + 1) * T], sq2[:, :],
                                         sgn[:, :]))
    nc.all_engine_barrier()
    if debug:
        with nc.Block() as blk:
            @blk.sync
            def _(sy):
                S['mi0'].dinc(sy.dma_start(dbg_wabs, wabs[:, :]))
                S['mi0'].dinc(sy.dma_start(dbg_thr, thr_b[:, :]))
                sy.wait_ge(S['mi0'].h, S['mi0'].n)
        nc.all_engine_barrier()
        bdg = {k: S[k].n for k in ('mi0', 'o0')}
        with nc.Block() as blk:
            @blk.vector
            def _(v):
                for k in range(NK):
                    if k >= 1:
                        v.wait_ge(S['o0'].h, bdg['o0'] + 16 * k)
                    S['mi0'].inc(v.tensor_copy(sqb[:, :], wq_sb[:, k * T:(k + 1) * T]))
            @blk.sync
            def _(sy):
                for k in range(NK):
                    sy.wait_ge(S['mi0'].h, bdg['mi0'] + k + 1)
                    S['o0'].dinc(sy.dma_start(
                        dbg_wq.rearrange("(k p) t -> k p t", p=128)[k], sqb[:, :]))
                sy.wait_ge(S['o0'].h, S['o0'].n)
        nc.all_engine_barrier()

    x1_t = x1T.rearrange("(k p) t -> k p t", p=128)
    x2_t = x2T.rearrange("(k p) t -> k p t", p=128)
    b = {k: S[k].n for k in ('px0', 'px1', 'mmt', 'dv', 'o0', 'o1')}
    osl = [sqb, sq2]
    with nc.psum_tensor("b_psB0", [128, T], F32) as psB0, \
         nc.psum_tensor("b_psB1", [128, T], F32) as psB1:
        psB4 = [psB0, psB1]
        with nc.Block() as blk:
            @blk.tensor
            def _(t):
                t.wait_ge(S['qz'].h, S['qz'].n)
                t.wait_ge(S['pap'].h, S['pap'].n)
                for m in range(NK):
                    if m >= 2:
                        t.wait_ge(S['dv'].h, b['dv'] + m - 1)
                    for n in range(NN):
                        for k in range(NK):
                            inst = t.matmul(
                                psB4[m % 2][:, n * 512:(n + 1) * 512],
                                wq_sb[:, k * T + m * 128:k * T + (m + 1) * 128],
                                xn_sb[:, k * T + n * 512:k * T + (n + 1) * 512],
                                start=(k == 0), stop=(k == NK - 1))
                    S['mmt'].inc(inst)
            @blk.vector
            def _(v):
                for m in range(NK):
                    v.wait_ge(S['mmt'].h, b['mmt'] + m + 1)
                    v.wait_ge(S[f'px{m % 2}'].h, b[f'px{m % 2}'] + 16 * (m // 2 + 1))
                    if m >= 2:
                        v.wait_ge(S[f'o{m % 2}'].h, b[f'o{m % 2}'] + 16 * (m // 2))
                    S['dv'].inc(v.tensor_tensor(osl[m % 2][:, :], psB4[m % 2][:, :],
                                                xb[m % 2][:, :], ALU.add))
            @blk.sync
            def _(sy):
                S['px0'].dinc(sy.dma_start(xb[0][:, :], x1_t[0]))
                S['px1'].dinc(sy.dma_start(xb[1][:, :], x1_t[1]))
                for m in range(NK):
                    sy.wait_ge(S['dv'].h, b['dv'] + m + 1)
                    S[f'o{m % 2}'].dinc(sy.dma_start(x2_t[m], osl[m % 2][:, :]))
                    if m + 2 < NK:
                        S[f'px{m % 2}'].dinc(sy.dma_start(xb[m % 2][:, :], x1_t[m + 2]))
                for s in ('o0', 'o1'):
                    sy.wait_ge(S[s].h, S[s].n)
    nc.all_engine_barrier()
    st.close()
    nc.all_engine_barrier()



def emit_sec_C(nc, S, sh, dr, debug=False):
    xb = sh['xb']; sqb = sh['sqb']; sq2 = sh['sq2']
    onc = sh['onc']; onr = sh['onr']; half_sb = sh.get('half_sb')
    st = contextlib.ExitStack()

    x2T = dr["x2T"]
    wqT = dr["wqT"]
    wkT = dr["wkT"]
    wvT = dr["wvT"]
    mem = dr["mem"]
    mnorm = dr["mnorm"]
    mweight = dr["mweight"]
    ones_col = dr["ones_col"]
    ones_row = dr["ones_row"]
    x3T = dr["x3T"]
    q_s = dr["qc_s"]
    q32_s = dr["qc32_s"]
    ad_s = dr["adc_s"]

    mem_sb = st.enter_context(nc.sbuf_tensor("c_mem_sb", [128, 4 * 512], BF16))
    mn_sb = st.enter_context(nc.sbuf_tensor("c_mn_sb", [128, 4], BF16))
    w_sb = st.enter_context(nc.sbuf_tensor("c_w_sb", [1, 2], F32))
    k_sb = st.enter_context(nc.sbuf_tensor("c_k_sb", [128, 4 * T], BF16))
    v_sb = st.enter_context(nc.sbuf_tensor("c_v_sb", [128, 16 * 520], BF16))
    srow = st.enter_context(nc.sbuf_tensor("c_srow", [1, 1024], F32))
    adst = st.enter_context(nc.sbuf_tensor("c_adst", [64, 1024], F32))


    with nc.Block() as blk:
        @blk.sync
        def _(sy):
            S['mi0'].dinc(sy.dma_start(onc[:, :], ones_col))
            S['mi0'].dinc(sy.dma_start(onr[:, :], ones_row))
            S['mi0'].dinc(sy.dma_start(w_sb[0:1, 0:1], mweight))
            sy.wait_ge(S['mi0'].h, S['mi0'].n)
        @blk.gpsimd
        def _(g):
            S['mi0'].dinc(g.dma_start(mn_sb[:, :], mnorm))
            S['mi0'].dinc(g.dma_start(
                mem_sb[:, :].rearrange("p (c m) -> p c m", m=512),
                mem.rearrange("(c p) m -> p c m", p=128)))
            g.wait_ge(S['mi0'].h, S['mi0'].n)
        @blk.vector
        def _(v):
            v.memset(v_sb[:, :], 1.0)
    nc.all_engine_barrier()
    with nc.Block() as blk:
        @blk.scalar
        def _(sc):
            S['prow'].inc(sc.activation(w_sb[0:1, 0:1], w_sb[0:1, 0:1], AF.Sigmoid))
        @blk.vector
        def _(v):
            v.wait_ge(S['prow'].h, S['prow'].n)
            S['prow'].inc(v.tensor_scalar(w_sb[0:1, 1:2], w_sb[0:1, 0:1], -1.0, 1.0,
                                          ALU.mult, ALU.add))
    nc.all_engine_barrier()

    c2 = contextlib.ExitStack()
    xh_sb = c2.enter_context(nc.sbuf_tensor("c_xh_sb", [128, NK * T], BF16))
    c2a = contextlib.ExitStack()
    row_m = c2a.enter_context(nc.sbuf_tensor("c_row_m", [1, T], F32))
    row_r1 = c2a.enter_context(nc.sbuf_tensor("c_row_r1", [1, T], F32))
    row_rho = c2a.enter_context(nc.sbuf_tensor("c_row_rho", [1, T], F32))
    row_mrho = c2a.enter_context(nc.sbuf_tensor("c_row_mrho", [1, T], F32))
    rows = {'m': row_m[0:1, :], 'r1': row_r1[0:1, :], 'rho': row_rho[0:1, :],
            'mrho': row_mrho[0:1, :], 't0': sqb[0:1, 0:T], 't1': sq2[0:1, 0:T]}
    bc0 = c2a.enter_context(nc.sbuf_tensor("c_bc0", [128, T], F32))
    bc1 = c2a.enter_context(nc.sbuf_tensor("c_bc1", [128, T], F32))
    emit_ln_stats(nc, S, x2T, xb, sqb, rows, onc, single=True, tag='p')
    emit_bcast(nc, S, [(rows['rho'], bc0), (rows['mrho'], bc1)], onr, tag='p')
    emit_ln_apply(nc, S, x2T, xb, bc0, bc1, xh_sb, None, None, tag='p')
    c2a.close()

    wbuf = [c2.enter_context(nc.sbuf_tensor(f"c_wbuf{i}", [128, NK * 128], BF16))
            for i in range(2)]
    wv_sb = c2.enter_context(nc.sbuf_tensor("c_wv_sb", [128, 16 * 512], BF16))
    qsl = [c2.enter_context(nc.sbuf_tensor(f"c_qsl{i}", [128, T], BF16)) for i in range(2)]
    q32sl = xb
    qs_t = q_s.rearrange("(k p) t -> k p t", p=128)
    q32_t = q32_s.rearrange("(k p) t -> k p t", p=128)

    b = {k: S[k].n for k in ('w0', 'w1', 'mmt', 'evt')}
    with nc.psum_tensor("c_psV0", [128, 512], F32) as psV0, \
         nc.psum_tensor("c_psV1", [128, 512], F32) as psV1:
        psV = [psV0, psV1]
        with nc.Block() as blk:
            @blk.gpsimd
            def _(g):
                S['w0'].dinc(g.dma_start(
                    wv_sb[:, :].rearrange("p (k c) -> p k c", c=512),
                    wvT.rearrange("(k p) c -> p k c", p=128)))
            @blk.tensor
            def _(t):
                t.wait_ge(S['pap'].h, S['pap'].n)
                t.wait_ge(S['w0'].h, b['w0'] + 16)
                for m in range(16):
                    if m >= 2:
                        t.wait_ge(S['evt'].h, b['evt'] + m - 1)
                    for k in range(NK):
                        inst = t.matmul(psV[m % 2][:, :],
                                        xh_sb[:, k * T + m * 128:k * T + (m + 1) * 128],
                                        wv_sb[:, k * 512:(k + 1) * 512],
                                        start=(k == 0), stop=(k == NK - 1))
                    S['mmt'].inc(inst)
            @blk.vector
            def _(v):
                for m in range(16):
                    v.wait_ge(S['mmt'].h, b['mmt'] + m + 1)
                    S['evt'].inc(v.tensor_copy(
                        v_sb[:, m * 520:(m + 1) * 520].rearrange(
                            "p (h c) -> p h c", c=65)[:, :, 0:64],
                        psV[m % 2][:, :].rearrange("p (h c) -> p h c", c=64)))
    nc.all_engine_barrier()

    b = {k: S[k].n for k in ('w0', 'w1', 'mmt', 'evt', 'o0', 'o1')}
    with nc.psum_tensor("c_psK0", [128, T], F32) as psK0, \
         nc.psum_tensor("c_psK1", [128, T], F32) as psK1:
        psK = [psK0, psK1]
        with nc.Block() as blk:
            @blk.gpsimd
            def _(g):
                for mi in range(20):
                    if mi >= 2:
                        g.wait_ge(S['mmt'].h, b['mmt'] + mi - 1)
                    src = wkT[mi] if mi < 4 else wqT[mi - 4]
                    S[f'w{mi % 2}'].dinc(g.dma_start(
                        wbuf[mi % 2][:, :].rearrange("p (k c) -> p k c", c=128),
                        src.rearrange("(k p) c -> p k c", p=128)))
            @blk.tensor
            def _(t):
                t.wait_ge(S['pap'].h, S['pap'].n)
                for mi in range(20):
                    t.wait_ge(S[f'w{mi % 2}'].h, b[f'w{mi % 2}'] + 16 * (mi // 2 + 1))
                    if mi >= 2:
                        t.wait_ge(S['evt'].h, b['evt'] + mi - 1)
                    for n in range(NN):
                        for k in range(NK):
                            inst = t.matmul(
                                psK[mi % 2][:, n * 512:(n + 1) * 512],
                                wbuf[mi % 2][:, k * 128:(k + 1) * 128],
                                xh_sb[:, k * T + n * 512:k * T + (n + 1) * 512],
                                start=(k == 0), stop=(k == NK - 1))
                    S['mmt'].inc(inst)
            @blk.scalar
            def _(sc):
                nq = 0
                for mi in range(20):
                    sc.wait_ge(S['mmt'].h, b['mmt'] + mi + 1)
                    if mi < 4:
                        S['evt'].inc(sc.activation(k_sb[:, mi * T:(mi + 1) * T],
                                                   psK[mi % 2][:, :], AF.Identity))
                    else:
                        if nq >= 2:
                            sc.wait_ge(S[f'o{nq % 2}'].h, b[f'o{nq % 2}'] + 32 * (nq // 2))
                        sc.activation(q32sl[nq % 2][:, :], psK[mi % 2][:, :], AF.Identity)
                        S['evt'].inc(sc.activation(qsl[nq % 2][:, :], psK[mi % 2][:, :],
                                                   AF.Identity))
                        nq += 1
            @blk.sync
            def _(sy):
                for nq in range(16):
                    mi = nq + 4
                    sy.wait_ge(S['evt'].h, b['evt'] + mi + 1)
                    S[f'o{nq % 2}'].dinc(sy.dma_start(qs_t[nq], qsl[nq % 2][:, :]))
                    S[f'o{nq % 2}'].dinc(sy.dma_start(q32_t[nq], q32sl[nq % 2][:, :]))
                for s2 in ('o0', 'o1'):
                    sy.wait_ge(S[s2].h, S[s2].n)
    nc.all_engine_barrier()
    c2.close()

    c3 = contextlib.ExitStack()
    qh = [c3.enter_context(nc.sbuf_tensor(f"c_qh{i}", [128, T], BF16)) for i in range(2)]
    etA = [c3.enter_context(nc.sbuf_tensor(f"c_etA{i}", [128, 512], BF16)) for i in range(2)]
    etB = [c3.enter_context(nc.sbuf_tensor(f"c_etB{i}", [128, 512], BF16)) for i in range(2)]
    srA = c3.enter_context(nc.sbuf_tensor("c_srA", [1, 512], F32))
    srB = c3.enter_context(nc.sbuf_tensor("c_srB", [1, 512], F32))
    adA = c3.enter_context(nc.sbuf_tensor("c_adA", [64, 512], F32))
    adB = c3.enter_context(nc.sbuf_tensor("c_adB", [64, 512], F32))
    ad_t = ad_s.rearrange("(g p) t -> g p t", p=64)
    NCH = 64
    b = {k: S[k].n for k in ('qh0', 'qh1', 'mmt', 'mmu', 'exp', 'evu',
                             'avm', 'avb', 'adv', 'advb', 'ads', 'adsb',
                             'evt', 'evtb', 'prow', 'prowb', 'pbc', 'pbcb')}
    with nc.psum_tensor("c_scA0", [128, 512], F32) as scA0, \
         nc.psum_tensor("c_scA1", [128, 512], F32) as scA1, \
         nc.psum_tensor("c_scB0", [128, 512], F32) as scB0, \
         nc.psum_tensor("c_scB1", [128, 512], F32) as scB1, \
         nc.psum_tensor("c_avA", [65, 512], F32) as avA, \
         nc.psum_tensor("c_avB", [65, 512], F32) as avB:
        scA = [scA0, scA1]; scB = [scB0, scB1]
        with nc.Block() as blk:
            @blk.gpsimd
            def _(g):
                for gp in range(16):
                    if gp >= 2:
                        g.wait_ge(S['mmt'].h, b['mmt'] + 64 * (gp - 1))
                    S[f'qh{gp % 2}'].dinc(g.dma_start(
                        qh[gp % 2][0:64, :],
                        q_s.rearrange("(g p) t -> g p t", p=64)[2 * gp]))
                    S[f'qh{gp % 2}'].dinc(g.dma_start(
                        qh[gp % 2][64:128, :],
                        q_s.rearrange("(g p) t -> g p t", p=64)[2 * gp + 1]))
            @blk.tensor
            def _(t):
                t.wait_ge(S['evt'].h, S['evt'].n - 0 if False else b['evt'])
                for gp in range(16):
                    gA = 2 * gp
                    qq, hA = divmod(gA, 8)
                    kch = hA // 2
                    t.wait_ge(S[f'qh{gp % 2}'].h, b[f'qh{gp % 2}'] + 32 * (gp // 2 + 1))
                    for sC in range(4):
                        it = gp * 4 + sC
                        soff = sC * 512
                        if it >= 1:
                            t.wait_ge(S['adv'].h, b['adv'] + it)
                            t.wait_ge(S['advb'].h, b['advb'] + it)
                        for tt in range(17):
                            g_idx = it * 16 + tt
                            if tt < 16:
                                if g_idx >= 2:
                                    t.wait_ge(S['exp'].h, b['exp'] + g_idx - 1)
                                    t.wait_ge(S['evu'].h, b['evu'] + g_idx - 1)
                                inst = t.matmul(
                                    scA[tt % 2][:, :],
                                    k_sb[:, kch * T + tt * 128:kch * T + (tt + 1) * 128][0:64, :],
                                    qh[gp % 2][0:64, soff:soff + 512],
                                    start=True, stop=True)
                                S['mmt'].inc(inst)
                                inst = t.matmul(
                                    scB[tt % 2][:, :],
                                    k_sb[:, kch * T + tt * 128:kch * T + (tt + 1) * 128][64:128, :],
                                    qh[gp % 2][64:128, soff:soff + 512],
                                    start=True, stop=True, tile_position=(64, 0))
                                S['mmu'].inc(inst)
                            if tt >= 1:
                                pt = tt - 1
                                t.wait_ge(S['exp'].h, b['exp'] + it * 16 + pt + 1)
                                inst = t.matmul(
                                    avA[:, :],
                                    v_sb[:, pt * 520 + hA * 65:pt * 520 + (hA + 1) * 65],
                                    etA[pt % 2][:, :], start=(pt == 0), stop=(pt == 15))
                                S['avm'].inc(inst)
                                t.wait_ge(S['evu'].h, b['evu'] + it * 16 + pt + 1)
                                inst = t.matmul(
                                    avB[:, :],
                                    v_sb[:, pt * 520 + (hA + 1) * 65:pt * 520 + (hA + 2) * 65],
                                    etB[pt % 2][:, :], start=(pt == 0), stop=(pt == 15))
                                S['avb'].inc(inst)
                        t.wait_ge(S['prow'].h, b['prow'] + it + 1)
                        t.wait_ge(S['evt'].h, b['evt'] + it + 1)
                        S['pbc'].inc(t.matmul(avA[0:64, :], onr[0:1, 0:64],
                                              srA[0:1, :], start=True, stop=True))
                        t.wait_ge(S['prowb'].h, b['prowb'] + it + 1)
                        t.wait_ge(S['evtb'].h, b['evtb'] + it + 1)
                        S['pbcb'].inc(t.matmul(avB[0:64, :], onr[0:1, 0:64],
                                               srB[0:1, :], start=True, stop=True))
            @blk.scalar
            def _(sc):
                for it in range(NCH):
                    for tt in range(16):
                        g_idx = it * 16 + tt
                        sc.wait_ge(S['mmt'].h, b['mmt'] + g_idx + 1)
                        if g_idx >= 2:
                            sc.wait_ge(S['avm'].h, b['avm'] + g_idx - 1)
                        S['exp'].inc(sc.activation(etA[tt % 2][:, :], scA[tt % 2][:, :],
                                                   AF.Exp, scale=0.125))
                        sc.wait_ge(S['mmu'].h, b['mmu'] + g_idx + 1)
                        if g_idx >= 2:
                            sc.wait_ge(S['avb'].h, b['avb'] + g_idx - 1)
                        S['evu'].inc(sc.activation(etB[tt % 2][:, :], scB[tt % 2][:, :],
                                                   AF.Exp, scale=0.125))
                    sc.wait_ge(S['avm'].h, b['avm'] + (it + 1) * 16)
                    if it >= 1:
                        sc.wait_ge(S['ads'].h, b['ads'] + 16 * it)
                    sc.activation(adA[:, :], avA[0:64, :], AF.Identity)
                    sc.activation(srA[0:1, :], avA[64:65, :], AF.Ln)
                    S['evt'].inc(sc.activation(srA[0:1, :], srA[0:1, :],
                                               AF.Exp, scale=-1.0))
                    sc.wait_ge(S['avb'].h, b['avb'] + (it + 1) * 16)
                    if it >= 1:
                        sc.wait_ge(S['adsb'].h, b['adsb'] + 16 * it)
                    sc.activation(adB[:, :], avB[0:64, :], AF.Identity)
                    sc.activation(srB[0:1, :], avB[64:65, :], AF.Ln)
                    S['evtb'].inc(sc.activation(srB[0:1, :], srB[0:1, :],
                                                AF.Exp, scale=-1.0))
            @blk.vector
            def _(v):
                for it in range(NCH):
                    v.wait_ge(S['evt'].h, b['evt'] + it + 1)
                    S['prow'].inc(v.tensor_single_scalar(srA[0:1, :], srA[0:1, :],
                                                         w_sb[0:1, 1:2], ALU.mult))
                    v.wait_ge(S['evtb'].h, b['evtb'] + it + 1)
                    S['prowb'].inc(v.tensor_single_scalar(srB[0:1, :], srB[0:1, :],
                                                          w_sb[0:1, 1:2], ALU.mult))
                    v.wait_ge(S['pbc'].h, b['pbc'] + it + 1)
                    S['adv'].inc(v.tensor_tensor(adA[:, :], adA[:, :],
                                                 avA[0:64, :], ALU.mult))
                    v.wait_ge(S['pbcb'].h, b['pbcb'] + it + 1)
                    S['advb'].inc(v.tensor_tensor(adB[:, :], adB[:, :],
                                                  avB[0:64, :], ALU.mult))
            @blk.sync
            def _(sy):
                for it in range(NCH):
                    gp, sC = divmod(it, 4)
                    sy.wait_ge(S['adv'].h, b['adv'] + it + 1)
                    S['ads'].dinc(sy.dma_start(
                        ad_t[2 * gp, :, sC * 512:(sC + 1) * 512], adA[:, :]))
                    sy.wait_ge(S['advb'].h, b['advb'] + it + 1)
                    S['adsb'].dinc(sy.dma_start(
                        ad_t[2 * gp + 1, :, sC * 512:(sC + 1) * 512], adB[:, :]))
                sy.wait_ge(S['ads'].h, S['ads'].n)
                sy.wait_ge(S['adsb'].h, S['adsb'].n)
    nc.all_engine_barrier()
    c3.close()

    c4 = contextlib.ExitStack()
    sg_sb = c4.enter_context(nc.sbuf_tensor("c_sg_sb", [128, 4 * T], BF16))
    bcw = c4.enter_context(nc.sbuf_tensor("c_bcw", [128, T], F32))
    adb = [c4.enter_context(nc.sbuf_tensor(f"c_adb{i}", [128, T], F32)) for i in range(2)]
    rows_t0 = sqb[0:1, 0:T]
    x2_t = x2T.rearrange("(k p) t -> k p t", p=128)
    x3_t = x3T.rearrange("(k p) t -> k p t", p=128)
    adf_t = ad_s.rearrange("(k p) t -> k p t", p=128)
    for qq in range(4):
        b = {k: S[k].n for k in ('px0', 'px1', 'psq', 'dv')}
        with nc.psum_tensor(f"c_dps{qq}", [1, T], F32) as dps:
            with nc.Block() as blk:
                @blk.sync
                def _(sy):
                    for c in range(4):
                        if c >= 2:
                            sy.wait_ge(S['dv'].h, b['dv'] + 2 * (c - 1))
                        S[f'px{c % 2}'].dinc(sy.dma_start(xb[c % 2][:, :],
                                                          q32_t[qq * 4 + c]))
                @blk.vector
                def _(v):
                    for c in range(4):
                        v.wait_ge(S[f'px{c % 2}'].h, b[f'px{c % 2}'] + 16 * (c // 2 + 1))
                        v.tensor_scalar_min(sqb[:, :], xb[c % 2][:, :], 0.0)
                        S['dv'].inc(v.tensor_scalar_max(sq2[:, :], xb[c % 2][:, :], 0.0))
                        v.wait_ge(S['psq'].h, b['psq'] + c + 1)
                        S['dv'].inc(v.scalar_tensor_tensor(
                            sg_sb[:, c * T:(c + 1) * T], bcw[:, :], -1.0, sq2[:, :],
                            ALU.add, ALU.add))
                @blk.scalar
                def _(sc):
                    for c in range(4):
                        sc.wait_ge(S['dv'].h, b['dv'] + 2 * c + 1)
                        S['psq'].inc(sc.activation(bcw[:, :], sqb[:, :], AF.Exp))
            nc.all_engine_barrier()
            b2 = {k: S[k].n for k in ('mmt', 'prow')}
            with nc.Block() as blk:
                @blk.tensor
                def _(t):
                    t.wait_ge(S['dv'].h, S['dv'].n)
                    for n in range(NN):
                        for kc in range(4):
                            inst = t.matmul(dps[:, n * 512:(n + 1) * 512],
                                            mn_sb[:, kc:kc + 1],
                                            sg_sb[:, kc * T + n * 512:kc * T + (n + 1) * 512],
                                            start=(kc == 0), stop=(kc == 3))
                    S['mmt'].inc(inst)
                @blk.vector
                def _(v):
                    v.wait_ge(S['mmt'].h, b2['mmt'] + 1)
                    v.reciprocal(rows_t0, dps[:, :])
                    S['prow'].inc(v.tensor_single_scalar(rows_t0, rows_t0,
                                                         w_sb[0:1, 0:1], ALU.mult))
            nc.all_engine_barrier()
        emit_bcast(nc, S, [(rows_t0, bcw)], onr, tag='p')
        b3 = {k: S[k].n for k in ('mmt', 'dv', 'o0', 'o1', 'px0', 'px1',
                                  'qh0', 'qh1')}
        with nc.psum_tensor(f"c_amps0_{qq}", [128, T], F32) as amps0, \
             nc.psum_tensor(f"c_amps1_{qq}", [128, T], F32) as amps1:
            amps = [amps0, amps1]
            with nc.Block() as blk:
                @blk.tensor
                def _(t):
                    t.wait_ge(S['dv'].h, S['dv'].n)
                    t.wait_ge(S['pbce'].h, S['pbce'].n)
                    for mt in range(4):
                        if mt >= 2:
                            t.wait_ge(S['dv'].h, b3['dv'] + mt - 1)
                        for n in range(NN):
                            for kc in range(4):
                                inst = t.matmul(
                                    amps[mt % 2][:, n * 512:(n + 1) * 512],
                                    mem_sb[:, kc * 512 + mt * 128:kc * 512 + (mt + 1) * 128],
                                    sg_sb[:, kc * T + n * 512:kc * T + (n + 1) * 512],
                                    start=(kc == 0), stop=(kc == 3))
                        S['mmt'].inc(inst)
                @blk.sync
                def _(sy):
                    for mt in range(2):
                        S[f'px{mt % 2}'].dinc(sy.dma_start(adb[mt % 2][:, :],
                                                           adf_t[qq * 4 + mt]))
                        S[f'qh{mt % 2}'].dinc(sy.dma_start(xb[mt % 2][:, :],
                                                           x2_t[qq * 4 + mt]))
                    for mt in range(4):
                        sy.wait_ge(S['dv'].h, b3['dv'] + mt + 1)
                        S[f'o{mt % 2}'].dinc(sy.dma_start(x3_t[qq * 4 + mt],
                                                          adb[mt % 2][:, :]))
                        if mt + 2 < 4:
                            S[f'px{mt % 2}'].dinc(sy.dma_start(
                                adb[mt % 2][:, :], adf_t[qq * 4 + mt + 2]))
                            S[f'qh{mt % 2}'].dinc(sy.dma_start(
                                xb[mt % 2][:, :], x2_t[qq * 4 + mt + 2]))
                    for s2 in ('o0', 'o1'):
                        sy.wait_ge(S[s2].h, S[s2].n)
                @blk.vector
                def _(v):
                    for mt in range(4):
                        v.wait_ge(S['mmt'].h, b3['mmt'] + mt + 1)
                        v.wait_ge(S[f'px{mt % 2}'].h, b3[f'px{mt % 2}'] + 16 * (mt // 2 + 1))
                        v.wait_ge(S[f'qh{mt % 2}'].h, b3[f'qh{mt % 2}'] + 16 * (mt // 2 + 1))
                        v.tensor_tensor(sq2[:, :], amps[mt % 2][:, :], bcw[:, :], ALU.mult)
                        v.tensor_add(sq2[:, :], sq2[:, :], xb[mt % 2][:, :])
                        S['dv'].inc(v.tensor_add(adb[mt % 2][:, :], adb[mt % 2][:, :],
                                                 sq2[:, :]))
        nc.all_engine_barrier()
    c4.close()
    st.close()
    nc.all_engine_barrier()



def emit_sec_D(nc, S, sh, dr, debug=False):
    xb = sh['xb']; sqb = sh['sqb']; sq2 = sh['sq2']
    onc = sh['onc']; onr = sh['onr']; half_sb = sh.get('half_sb')
    st = contextlib.ExitStack()
    TQ = 512
    NQ = 4

    x3T = dr["x3T"]
    gwT = dr["gwT"]
    swg = dr["swg"]
    swu = dr["swu"]
    swd = dr["swd"]
    eg = dr["eg"]
    eu = dr["eu"]
    ed = dr["ed"]
    ones_col = dr["ones_col"]
    ones_row = dr["ones_row"]
    idt = dr["idt"]
    outT = dr["outT"]
    werow_s = dr["werow_s"]
    if debug:
        dbg_we = dr["dbg_we"]
        dbg_g = dr["dbg_g"]
        dbg_wsb = dr["dbg_wsb"]

    row_m = st.enter_context(nc.sbuf_tensor("d_row_m", [1, T], F32))
    row_r1 = st.enter_context(nc.sbuf_tensor("d_row_r1", [1, T], F32))
    row_rho = st.enter_context(nc.sbuf_tensor("d_row_rho", [1, T], F32))
    row_mrho = st.enter_context(nc.sbuf_tensor("d_row_mrho", [1, T], F32))
    rows = {'m': row_m[0:1, :], 'r1': row_r1[0:1, :], 'rho': row_rho[0:1, :],
            'mrho': row_mrho[0:1, :], 't0': sqb[0:1, 0:T], 't1': sq2[0:1, 0:T]}
    bc0 = st.enter_context(nc.sbuf_tensor("d_bc0", [128, T], F32))
    bc1 = st.enter_context(nc.sbuf_tensor("d_bc1", [128, T], F32))
    idt_sb = st.enter_context(nc.sbuf_tensor("d_idt_sb", [128, 128], F32))
    gw_sb = st.enter_context(nc.sbuf_tensor("d_gw_sb", [128, 16 * 8], BF16))
    xn_sb = st.enter_context(nc.sbuf_tensor("d_xn_sb", [128, NK * TQ], BF16))
    acc = st.enter_context(nc.sbuf_tensor("d_acc", [128, 16 * TQ], F32))
    hs_sb = st.enter_context(nc.sbuf_tensor("d_hs_sb", [128, 22 * TQ], BF16))
    wgb = [st.enter_context(nc.sbuf_tensor(f"d_wgb{i}", [128, D], BF16)) for i in range(2)]
    wub = [st.enter_context(nc.sbuf_tensor(f"d_wub{i}", [128, D], BF16)) for i in range(2)]
    wdb = [st.enter_context(nc.sbuf_tensor(f"d_wdb{i}", [128, 2816], BF16)) for i in range(2)]
    sil = [st.enter_context(nc.sbuf_tensor(f"d_sil{i}", [128, TQ], BF16)) for i in range(2)]
    g_sb = st.enter_context(nc.sbuf_tensor("d_g_sb", [128, 32], F32))
    we_sb = st.enter_context(nc.sbuf_tensor("d_we_sb", [128, 32], F32))
    m14 = st.enter_context(nc.sbuf_tensor("d_m14", [128, 4], F32))
    m24 = st.enter_context(nc.sbuf_tensor("d_m24", [128, 4], F32))
    ms4 = st.enter_context(nc.sbuf_tensor("d_ms4", [128, 4], F32))
    rs4 = st.enter_context(nc.sbuf_tensor("d_rs4", [128, 4], F32))
    negm = st.enter_context(nc.sbuf_tensor("d_negm", [128, 32], F32))
    mkb = st.enter_context(nc.sbuf_tensor("d_mkb", [128, 32], F32))
    mk2 = st.enter_context(nc.sbuf_tensor("d_mk2", [128, 32], F32))
    wtmp = st.enter_context(nc.sbuf_tensor("d_wtmp", [1, TQ], F32))
    osl2 = [st.enter_context(nc.sbuf_tensor(f"d_osl2{i}", [128, TQ], F32))
            for i in range(2)]
    bcwe = st.enter_context(nc.sbuf_tensor("d_bcwe", [128, TQ], F32))


    with nc.Block() as blk:
        @blk.sync
        def _(sy):
            S['mi0'].dinc(sy.dma_start(onc[:, :], ones_col))
            S['mi0'].dinc(sy.dma_start(onr[:, :], ones_row))
            S['mi0'].dinc(sy.dma_start(idt_sb[:, :], idt))
            sy.wait_ge(S['mi0'].h, S['mi0'].n)
        @blk.gpsimd
        def _(g):
            S['mi0'].dinc(g.dma_start(
                gw_sb[:, :].rearrange("p (k c) -> p k c", c=8),
                gwT.rearrange("(k p) c -> p k c", p=128)))
            g.wait_ge(S['mi0'].h, S['mi0'].n)
    nc.all_engine_barrier()

    emit_ln_stats(nc, S, x3T, xb, sqb, rows, onc, single=True, tag='p')
    emit_bcast(nc, S, [(rows['rho'], bc0), (rows['mrho'], bc1)], onr, tag='p')

    x3_t = x3T.rearrange("(k p) t -> k p t", p=128)
    out_t = outT.rearrange("(k p) t -> k p t", p=128)

    for Q in range(NQ):
        off = Q * TQ
        emit_ln_apply(nc, S, x3T, xb, bc0, bc1, xn_sb, None, None, tag='p',
                      half=None, seg=(off, TQ))

        b = {k: S[k].n for k in ('mmt', 'evt', 'dv', 'gp', 'prow', 'tp',
                                 'pbc', 'wr0', 'psq')}
        with nc.psum_tensor(f"d_gps{Q}", [128, 8], F32) as gps, \
             nc.psum_tensor(f"d_tpps{Q}", [1, 128], F32) as tpps:
            with nc.Block() as blk:
                @blk.tensor
                def _(t):
                    t.wait_ge(S['pap'].h, S['pap'].n)
                    for tt in range(4):
                        if tt >= 1:
                            t.wait_ge(S['evt'].h, b['evt'] + tt)
                        for k in range(NK):
                            inst = t.matmul(gps[:, :],
                                            xn_sb[:, k * TQ + tt * 128:k * TQ + (tt + 1) * 128],
                                            gw_sb[:, k * 8:(k + 1) * 8],
                                            start=(k == 0), stop=(k == NK - 1))
                        S['mmt'].inc(inst)
                    for e in range(8):
                        for tt in range(4):
                            t.wait_ge(S['gp'].h, b['gp'] + 16)
                            if e + tt > 0:
                                t.wait_ge(S['prow'].h, b['prow'] + 4 * e + tt)
                            S['tp'].inc(t.transpose(tpps[:, :],
                                                    we_sb[:, tt * 8 + e:tt * 8 + e + 1],
                                                    idt_sb[:, :]))
                @blk.scalar
                def _(sc):
                    for tt in range(4):
                        sc.wait_ge(S['mmt'].h, b['mmt'] + tt + 1)
                        S['evt'].inc(sc.activation(g_sb[:, tt * 8:(tt + 1) * 8],
                                                   gps[:, :], AF.Sigmoid))
                    sc.wait_ge(S['dv'].h, b['dv'] + 12)
                    for tt in range(4):
                        S['psq'].inc(sc.activation(ms4[:, tt:tt + 1],
                                                   m24[:, tt:tt + 1],
                                                   AF.Identity, bias=m14[:, tt:tt + 1]))
                @blk.vector
                def _(v):
                    for tt in range(4):
                        v.wait_ge(S['evt'].h, b['evt'] + tt + 1)
                        S['dv'].inc(v.reduce_max(m14[:, tt:tt + 1],
                                                 g_sb[:, tt * 8:(tt + 1) * 8], axis=AX))
                    v.wait_ge(S['gp'].h, b['gp'] + 4)
                    for tt in range(4):
                        S['dv'].inc(v.tensor_add(mkb[:, tt * 8:(tt + 1) * 8],
                                                 g_sb[:, tt * 8:(tt + 1) * 8],
                                                 negm[:, tt * 8:(tt + 1) * 8]))
                    v.wait_ge(S['gp'].h, b['gp'] + 8)
                    for tt in range(4):
                        S['dv'].inc(v.reduce_max(m24[:, tt:tt + 1],
                                                 mk2[:, tt * 8:(tt + 1) * 8], axis=AX))
                    v.wait_ge(S['psq'].h, b['psq'] + 4)
                    S['dv'].inc(v.reciprocal(rs4[:, :], ms4[:, :]))
                    v.wait_ge(S['gp'].h, b['gp'] + 12)
                    for tt in range(4):
                        S['dv'].inc(v.tensor_mul(mkb[:, tt * 8:(tt + 1) * 8],
                                                 g_sb[:, tt * 8:(tt + 1) * 8],
                                                 negm[:, tt * 8:(tt + 1) * 8]))
                    for e in range(8):
                        if e >= 1:
                            v.wait_ge(S['wr0'].h, b['wr0'] + 16 * e)
                        for tt in range(4):
                            v.wait_ge(S['tp'].h, b['tp'] + 4 * e + tt + 1)
                            S['prow'].inc(v.tensor_copy(
                                wtmp[0:1, tt * 128:(tt + 1) * 128], tpps[:, :]))
                @blk.gpsimd
                def _(g):
                    g.wait_ge(S['dv'].h, b['dv'] + 4)
                    for tt in range(4):
                        S['gp'].inc(g.tensor_scalar(negm[:, tt * 8:(tt + 1) * 8],
                                                    g_sb[:, tt * 8:(tt + 1) * 8],
                                                    m14[:, tt:tt + 1], -1.0e30,
                                                    ALU.is_equal, ALU.mult))
                    g.wait_ge(S['dv'].h, b['dv'] + 8)
                    for tt in range(4):
                        S['gp'].inc(g.tensor_copy(mk2[:, tt * 8:(tt + 1) * 8],
                                                  mkb[:, tt * 8:(tt + 1) * 8]))
                    g.wait_ge(S['dv'].h, b['dv'] + 12)
                    for tt in range(4):
                        S['gp'].inc(g.tensor_single_scalar(
                            negm[:, tt * 8:(tt + 1) * 8],
                            g_sb[:, tt * 8:(tt + 1) * 8], m24[:, tt:tt + 1], ALU.is_ge))
                    g.wait_ge(S['dv'].h, b['dv'] + 17)
                    for tt in range(4):
                        S['gp'].inc(g.tensor_scalar(we_sb[:, tt * 8:(tt + 1) * 8],
                                                    mkb[:, tt * 8:(tt + 1) * 8],
                                                    rs4[:, tt:tt + 1], 2.5,
                                                    ALU.mult, ALU.mult))
                @blk.sync
                def _(sy):
                    for e in range(8):
                        sy.wait_ge(S['prow'].h, b['prow'] + 4 * e + 4)
                        S['wr0'].dinc(sy.dma_start(werow_s[e:e + 1, off:off + TQ],
                                                   wtmp[0:1, :]))
                    if debug:
                        sy.wait_ge(S['gp'].h, b['gp'] + 16)
                        S['wr0'].dinc(sy.dma_start(dbg_g[Q], g_sb[:, :]))
                        S['wr0'].dinc(sy.dma_start(dbg_wsb[Q], we_sb[:, :]))
                    sy.wait_ge(S['wr0'].h, S['wr0'].n)
        nc.all_engine_barrier()

        phases = [('s', 22, 16)] + [(e, 11, 16) for e in range(8)]
        g_of = []
        d_of = []
        for pi, (ph, nmt, nm) in enumerate(phases):
            for mt in range(nmt):
                g_of.append((pi, mt))
            for m in range(nm):
                d_of.append((pi, m))
        NG = len(g_of)
        ND = len(d_of)
        hreq = []
        c = 0
        for pi, (ph, nmt, nm) in enumerate(phases):
            c += nmt
            hreq.append(c)
        b = {k: S[k].n for k in ('w0', 'w1', 'u0', 'u1', 'd0', 'd1', 'mmt',
                                 'mmu', 'mmd', 'evt', 'dvh', 'dva', 'pbc',
                                 'gp', 'wr0', 'px0', 'px1', 'o0', 'o1')}
        with nc.psum_tensor(f"d_psG0_{Q}", [128, TQ], F32) as psG0, \
             nc.psum_tensor(f"d_psG1_{Q}", [128, TQ], F32) as psG1, \
             nc.psum_tensor(f"d_psU0_{Q}", [128, TQ], F32) as psU0, \
             nc.psum_tensor(f"d_psU1_{Q}", [128, TQ], F32) as psU1, \
             nc.psum_tensor(f"d_psD0_{Q}", [128, TQ], F32) as psD0, \
             nc.psum_tensor(f"d_psD1_{Q}", [128, TQ], F32) as psD1, \
             nc.psum_tensor(f"d_psBc_{Q}", [128, TQ], F32) as psBc:
            psG = [psG0, psG1]; psU = [psU0, psU1]; psDl = [psD0, psD1]
            def g_src(pi, mt):
                ph = phases[pi][0]
                if ph == 's':
                    return swg[mt], swu[mt]
                return eg[ph, mt], eu[ph, mt]
            def d_src(pi, m):
                ph = phases[pi][0]
                return swd[m] if ph == 's' else ed[ph, m]
            with nc.Block() as blk:
                @blk.gpsimd
                def _(g):
                    j = 0
                    jj = 0
                    for pi, (ph, nmt, nm) in enumerate(phases):
                        for mt in range(nmt):
                            if j >= 2:
                                g.wait_ge(S['mmt'].h, b['mmt'] + j - 1)
                            sg, su = g_src(pi, mt)
                            S[f'w{j % 2}'].dinc(g.dma_start(
                                wgb[j % 2][:, :].rearrange("p (k c) -> p k c", c=128),
                                sg.rearrange("(k p) c -> p k c", p=128)))
                            if j >= 2:
                                g.wait_ge(S['mmu'].h, b['mmu'] + j - 1)
                            S[f'u{j % 2}'].dinc(g.dma_start(
                                wub[j % 2][:, :].rearrange("p (k c) -> p k c", c=128),
                                su.rearrange("(k p) c -> p k c", p=128)))
                            j += 1
                        for m in range(nm):
                            if jj >= 2:
                                g.wait_ge(S['mmd'].h, b['mmd'] + jj - 1)
                            sd = d_src(pi, m)
                            kw = 22 if ph == 's' else 11
                            S[f'd{jj % 2}'].dinc(g.dma_start(
                                wdb[jj % 2][:, 0:kw * 128].rearrange(
                                    "p (k c) -> p k c", c=128),
                                sd.rearrange("(k p) c -> p k c", p=128)))
                            jj += 1
                @blk.sync
                def _(sy):
                    for e in range(8):
                        sy.wait_ge(S['pbc'].h, b['pbc'] + e)
                        S['wr0'].dinc(sy.dma_start(wtmp[0:1, :],
                                                   werow_s[e:e + 1, off:off + TQ]))
                    S['px0'].dinc(sy.dma_start(xb[0][:, 0:TQ],
                                               x3_t[0, :, off:off + TQ]))
                    S['px1'].dinc(sy.dma_start(xb[1][:, 0:TQ],
                                               x3_t[1, :, off:off + TQ]))
                    for m in range(16):
                        sy.wait_ge(S['dva'].h, b['dva'] + 128 + m + 1)
                        S[f'o{m % 2}'].dinc(sy.dma_start(
                            out_t[m, :, off:off + TQ], sq2[:, m % 2 * TQ:(m % 2 + 1) * TQ]
                            if False else (sqb[:, 0:TQ] if False else
                                           osl2[m % 2][:, 0:TQ])))
                        if m + 2 < 16:
                            S[f'px{m % 2}'].dinc(sy.dma_start(
                                xb[m % 2][:, 0:TQ], x3_t[m + 2, :, off:off + TQ]))
                    for s2 in ('o0', 'o1'):
                        sy.wait_ge(S[s2].h, S[s2].n)
                @blk.tensor
                def _(t):
                    t.wait_ge(S['pap'].h, S['pap'].n)
                    jj = 0
                    for pi, (ph, nmt, nm) in enumerate(phases):
                        if ph != 's':
                            e = ph
                            t.wait_ge(S['wr0'].h, b['wr0'] + 16 * (e + 1))
                            t.wait_ge(S['gp'].h, b['gp'] + e)
                            S['pbc'].inc(t.matmul(psBc[:, :], onr[:, :], wtmp[0:1, :],
                                                  start=True, stop=True))
                        for mt in range(nmt):
                            j = len([0 for p2 in range(pi) for _ in range(phases[p2][1])]) + mt
                            t.wait_ge(S[f'w{j % 2}'].h, b[f'w{j % 2}'] + 16 * (j // 2 + 1))
                            if j >= 2:
                                t.wait_ge(S['evt'].h, b['evt'] + j - 1)
                            for k in range(NK):
                                inst = t.matmul(psG[j % 2][:, :],
                                                wgb[j % 2][:, k * 128:(k + 1) * 128],
                                                xn_sb[:, k * TQ:(k + 1) * TQ],
                                                start=(k == 0), stop=(k == NK - 1))
                            S['mmt'].inc(inst)
                            t.wait_ge(S[f'u{j % 2}'].h, b[f'u{j % 2}'] + 16 * (j // 2 + 1))
                            if j >= 2:
                                t.wait_ge(S['dvh'].h, b['dvh'] + j - 1)
                            for k in range(NK):
                                inst = t.matmul(psU[j % 2][:, :],
                                                wub[j % 2][:, k * 128:(k + 1) * 128],
                                                xn_sb[:, k * TQ:(k + 1) * TQ],
                                                start=(k == 0), stop=(k == NK - 1))
                            S['mmu'].inc(inst)
                        for m in range(nm):
                            kw = 22 if ph == 's' else 11
                            t.wait_ge(S[f'd{jj % 2}'].h, b[f'd{jj % 2}'] + 16 * (jj // 2 + 1))
                            t.wait_ge(S['dvh'].h, b['dvh'] + hreq[pi])
                            if jj >= 2:
                                t.wait_ge(S['dva'].h, b['dva'] + jj - 1)
                            for k in range(kw):
                                inst = t.matmul(psDl[jj % 2][:, :],
                                                wdb[jj % 2][:, k * 128:(k + 1) * 128],
                                                hs_sb[:, k * TQ:(k + 1) * TQ],
                                                start=(k == 0), stop=(k == kw - 1))
                            S['mmd'].inc(inst)
                            jj += 1
                @blk.scalar
                def _(sc):
                    for j in range(NG):
                        sc.wait_ge(S['mmt'].h, b['mmt'] + j + 1)
                        if j >= 2:
                            sc.wait_ge(S['dvh'].h, b['dvh'] + j - 1)
                        S['evt'].inc(sc.activation(sil[j % 2][:, 0:TQ],
                                                   psG[j % 2][:, :], AF.Silu))
                @blk.vector
                def _(v):
                    jj = 0
                    j = 0
                    for pi, (ph, nmt, nm) in enumerate(phases):
                        if ph != 's':
                            v.wait_ge(S['pbc'].h, b['pbc'] + ph + 1)
                            S['gp'].inc(v.tensor_copy(bcwe[:, 0:TQ], psBc[:, :]))
                        for mt in range(nmt):
                            v.wait_ge(S['evt'].h, b['evt'] + j + 1)
                            v.wait_ge(S['mmu'].h, b['mmu'] + j + 1)
                            if ph == 's':
                                S['dvh'].inc(v.tensor_tensor(
                                    hs_sb[:, mt * TQ:(mt + 1) * TQ],
                                    sil[j % 2][:, 0:TQ], psU[j % 2][:, :], ALU.mult))
                            else:
                                v.tensor_tensor(sqb[:, 0:TQ], sil[j % 2][:, 0:TQ],
                                                psU[j % 2][:, :], ALU.mult)
                                S['dvh'].inc(v.tensor_tensor(
                                    hs_sb[:, mt * TQ:(mt + 1) * TQ],
                                    sqb[:, 0:TQ], bcwe[:, 0:TQ], ALU.mult))
                            j += 1
                        for m in range(nm):
                            v.wait_ge(S['mmd'].h, b['mmd'] + jj + 1)
                            if ph == 's':
                                S['dva'].inc(v.tensor_copy(acc[:, m * TQ:(m + 1) * TQ],
                                                           psDl[jj % 2][:, :]))
                            elif ph != 7:
                                S['dva'].inc(v.tensor_tensor(
                                    acc[:, m * TQ:(m + 1) * TQ], psDl[jj % 2][:, :],
                                    acc[:, m * TQ:(m + 1) * TQ], ALU.add))
                            else:
                                v.wait_ge(S[f'px{m % 2}'].h,
                                          b[f'px{m % 2}'] + 16 * (m // 2 + 1))
                                if m >= 2:
                                    v.wait_ge(S[f'o{m % 2}'].h,
                                              b[f'o{m % 2}'] + 16 * (m // 2))
                                v.tensor_tensor(acc[:, m * TQ:(m + 1) * TQ],
                                                psDl[jj % 2][:, :],
                                                acc[:, m * TQ:(m + 1) * TQ], ALU.add)
                                S['dva'].inc(v.tensor_tensor(
                                    osl2[m % 2][:, 0:TQ],
                                    acc[:, m * TQ:(m + 1) * TQ],
                                    xb[m % 2][:, 0:TQ], ALU.add))
                            jj += 1
        nc.all_engine_barrier()

        nc.all_engine_barrier()
    if debug:
        with nc.Block() as blk:
            @blk.sync
            def _(sy):
                S['mi0'].dinc(sy.dma_start(dbg_we, werow_s))
                sy.wait_ge(S['mi0'].h, S['mi0'].n)
        nc.all_engine_barrier()
    st.close()
    nc.all_engine_barrier()



def perm_pi():
    return np.concatenate([np.arange(0, 1024, 2), np.arange(1, 1024, 2),
                           np.arange(1024, 2048)])




def prep_A(X, W_in, b_in, W_out, b_out, gamma, beta):
    pi = perm_pi()
    colperm = np.concatenate([pi, 2048 + pi, 4096 + pi])
    W_in_p = np.ascontiguousarray(W_in[:, colperm])
    b_in_p = np.ascontiguousarray(b_in[colperm])
    W_out_p = np.ascontiguousarray(W_out[pi, :])
    w_in_blocks = np.ascontiguousarray(
        W_in_p.reshape(2048, 48, 128).transpose(1, 0, 2))
    b_in_t = np.ascontiguousarray(b_in_p.reshape(48, 128).T)
    b_out_t = np.ascontiguousarray(b_out.reshape(16, 128).T)
    gamma_t = np.ascontiguousarray(gamma.reshape(16, 128).T)
    beta_t = np.ascontiguousarray(beta.reshape(16, 128).T)
    inv_freq = (1.0 / 10000.0 ** (np.arange(0, 1024, 2).astype(np.float64) / 1024.0))
    freqs = np.arange(T, dtype=np.float64)[None, :] * inv_freq[:, None]
    cosT = np.cos(freqs).astype(np.float32)
    sinT = np.sin(freqs).astype(np.float32)
    shared = {
        "w_in": w_in_blocks.astype(np.float32), "b_in": b_in_t.astype(np.float32),
        "w_out": W_out_p.astype(np.float32), "b_out": b_out_t.astype(np.float32),
        "gamma": gamma_t.astype(np.float32), "beta": beta_t.astype(np.float32),
        "cosT": cosT, "sinT": sinT,
        "ones_col": np.ones((128, 1), np.float32),
        "ones_row": np.ones((1, 128), np.float32),
    }
    maps = []
    for bb in range(8):
        m = dict(shared)
        m["xT"] = np.ascontiguousarray(X[bb].T)
        maps.append(m)
    return maps





def prep_B(W, ln_w, ln_b):
    shared = {
        "w": np.ascontiguousarray(W.astype(np.float32)),
        "lnw": np.ascontiguousarray(ln_w.reshape(16, 128).T).astype(np.float32),
        "lnb": np.ascontiguousarray(ln_b.reshape(16, 128).T).astype(np.float32),
        "ones_col": np.ones((128, 1), np.float32),
        "ones_row": np.ones((1, 128), np.float32),
    }
    return shared





def prep_C(Wq, Wk, Wv, memory, memory_norm, memory_weight):
    wqT = np.ascontiguousarray(Wq.T)
    wkT = np.ascontiguousarray(Wk.T)
    wvT = np.ascontiguousarray(Wv.T)
    return {
        "wqT": np.ascontiguousarray(wqT.reshape(2048, 16, 128).transpose(1, 0, 2)),
        "wkT": np.ascontiguousarray(wkT.reshape(2048, 4, 128).transpose(1, 0, 2)),
        "wvT": wvT.astype(np.float32),
        "mem": memory.astype(np.float32),
        "mnorm": np.ascontiguousarray(memory_norm.reshape(4, 128).T).astype(np.float32),
        "mweight": memory_weight.reshape(1, 1).astype(np.float32),
        "ones_col": np.ones((128, 1), np.float32),
        "ones_row": np.ones((1, 128), np.float32),
    }





def prep_D(gate_w, shared_wg, shared_wu, shared_wd, exp_wg, exp_wu, exp_wd):
    def blocks(WT, nb):
        K, M = WT.shape
        return np.ascontiguousarray(WT.reshape(K, nb, 128).transpose(1, 0, 2))
    swgT = np.ascontiguousarray(shared_wg.T)
    swuT = np.ascontiguousarray(shared_wu.T)
    swdT = np.ascontiguousarray(shared_wd.T)
    egT = np.stack([blocks(np.ascontiguousarray(exp_wg[e].T), 11) for e in range(8)])
    euT = np.stack([blocks(np.ascontiguousarray(exp_wu[e].T), 11) for e in range(8)])
    edT = np.stack([blocks(np.ascontiguousarray(exp_wd[e].T), 16) for e in range(8)])
    return {
        "gwT": np.ascontiguousarray(gate_w.T).astype(np.float32),
        "swg": blocks(swgT, 22).astype(np.float32),
        "swu": blocks(swuT, 22).astype(np.float32),
        "swd": blocks(swdT, 16).astype(np.float32),
        "eg": egT.astype(np.float32), "eu": euT.astype(np.float32),
        "ed": edT.astype(np.float32),
        "ones_col": np.ones((128, 1), np.float32),
        "ones_row": np.ones((1, 128), np.float32),
        "idt": np.eye(128, dtype=np.float32),
    }




ALL_SEMS = ['px0', 'px1', 'psq', 'pmm', 'prow', 'pap', 'pbc', 'pbce',
            'w0', 'w1', 'u0', 'u1', 'd0', 'd1', 'mmt', 'mmu', 'evt', 'evu',
            'dv', 'gp', 'mi0', 'o0', 'o1', 'q0', 'q1', 'k0', 'r0', 'r1s',
            'cs0', 'rot', 'qh0', 'qh1', 'exp', 'avm', 'adv', 'ads', 'tp', 'wr0', 'qz', 'mmd', 'dvh', 'dva', 'avb', 'advb', 'adsb', 'evtb', 'prowb', 'pbcb']


def build_full(debug=False):
    nc = bass.Bass("TRN2", target_bir_lowering=False)
    cleanup = nc.cleanup_on_exit()
    cleanup.__enter__()
    top = contextlib.ExitStack()

    dr = {}
    def din(name, shape):
        dr[name] = nc.dram_tensor(name, shape, F32, kind="ExternalInput").ap()
    din("xT", [D, T]); din("w_in", [48, D, 128]); din("b_in", [128, 48])
    din("w_out", [D, D]); din("b_out", [128, 16]); din("gamma", [128, 16])
    din("beta", [128, 16]); din("cosT", [512, T]); din("sinT", [512, T])
    din("ones_col", [128, 1]); din("ones_row", [1, 128])
    din("w", [D, D]); din("lnw", [128, 16]); din("lnb", [128, 16])
    din("wqT", [16, D, 128]); din("wkT", [4, D, 128]); din("wvT", [D, 512])
    din("mem", [512, 512]); din("mnorm", [128, 4]); din("mweight", [1, 1])
    din("gwT", [D, 8]); din("swg", [22, D, 128]); din("swu", [22, D, 128])
    din("swd", [16, 2816, 128]); din("idt", [128, 128])
    dr["eg"] = nc.dram_tensor("eg", [8, 11, D, 128], F32, kind="ExternalInput").ap()
    dr["eu"] = nc.dram_tensor("eu", [8, 11, D, 128], F32, kind="ExternalInput").ap()
    dr["ed"] = nc.dram_tensor("ed", [8, 16, 1408, 128], F32, kind="ExternalInput").ap()
    dr["outT"] = nc.dram_tensor("outT", [D, T], F32, kind="ExternalOutput").ap()
    for nm, shp, dt in (("x1T", [D, T], F32), ("x2T", [D, T], F32),
                        ("x3T", [D, T], F32), ("qa_s", [D, T], BF16),
                        ("ka_s", [D, T], F32), ("va_s", [D, T], F32),
                        ("qc_s", [D, T], BF16), ("qc32_s", [D, T], F32),
                        ("adc_s", [D, T], F32), ("werow_s", [8, T], F32)):
        dr[nm] = nc.dram_tensor(nm, shp, dt, kind="Internal").ap()

    sh = {}
    sh['xb'] = [top.enter_context(nc.sbuf_tensor(f"sh_xb{i}", [128, T], F32))
                for i in range(2)]
    sh['sqb'] = top.enter_context(nc.sbuf_tensor("sh_sqb", [128, T], F32))
    sh['sq2'] = top.enter_context(nc.sbuf_tensor("sh_sq2", [128, T], F32))
    sh['onc'] = top.enter_context(nc.sbuf_tensor("sh_onc", [128, 1], F32))
    sh['onr'] = top.enter_context(nc.sbuf_tensor("sh_onr", [1, 128], F32))
    sh['half_sb'] = top.enter_context(nc.sbuf_tensor("sh_half", [128, 1], F32))

    S = sems(nc, top, ALL_SEMS)
    emit_sec_A(nc, S, sh, dr, debug=debug)
    emit_sec_B(nc, S, sh, dr, debug=debug)
    emit_sec_C(nc, S, sh, dr, debug=debug)
    emit_sec_D(nc, S, sh, dr, debug=debug)

    top.close()
    nc.all_engine_barrier()
    cleanup.__exit__(None, None, None)
    return nc


def prep_full(inputs):
    mapsA = prep_A(inputs["L0_attn_X"], inputs["L0_attn_W_in"], inputs["L0_attn_b_in"],
                   inputs["L0_attn_W_out"], inputs["L0_attn_b_out"],
                   inputs["L0_attn_gamma"], inputs["L0_attn_beta"])
    sB = prep_B(inputs["L0_ffn_W"], inputs["L0_ffn_ln_w"], inputs["L0_ffn_ln_b"])
    sC = prep_C(inputs["L1_attn_Wq"], inputs["L1_attn_Wk"], inputs["L1_attn_Wv"],
                inputs["L1_attn_memory"], inputs["L1_attn_memory_norm"],
                inputs["L1_attn_memory_weight"])
    sD = prep_D(inputs["L1_ffn_gate_w"], inputs["L1_ffn_shared_wg"],
                inputs["L1_ffn_shared_wu"], inputs["L1_ffn_shared_wd"],
                inputs["L1_ffn_exp_wg"], inputs["L1_ffn_exp_wu"],
                inputs["L1_ffn_exp_wd"])
    maps = []
    for b in range(8):
        m = dict(mapsA[b]); m.update(sB); m.update(sC); m.update(sD)
        maps.append(m)
    return maps


def kernel(**inputs):
    maps = prep_full(inputs)
    nc = build_full()
    res = run_bass_kernel_spmd(nc, maps, core_ids=list(range(8)))
    out = np.stack([res.results[b]["outT"].T for b in range(8)], axis=0)
    if not np.isfinite(out).all():
        res = run_bass_kernel_spmd(nc, maps, core_ids=list(range(8)))
        out = np.stack([res.results[b]["outT"].T for b in range(8)], axis=0)
    return np.ascontiguousarray(out)
